# version 09 (final)
# baseline (speedup 1.0000x reference)
import contextlib
import numpy as np
import concourse.bass as bass
import concourse.mybir as mybir
from concourse.bass_utils import run_bass_kernel_spmd

F32 = mybir.dt.float32
BF16 = mybir.dt.bfloat16
U32 = mybir.dt.uint32
AF = mybir.ActivationFunctionType
ALU = mybir.AluOpType
AX = mybir.AxisListType
AP = bass.AP

NCORES = 8
T = 4096
NT = T // 128
D = 1024
H = 16
HD = 64
E = 8
FF = 4096
CAP = 1280
NSLOT = E * CAP
BIG = 2.0e9

DEBUG = False
BUILD_STAGE = 9


def build():
    nc = bass.Bass("TRN2", target_bir_lowering=False)

    x_in = nc.dram_tensor("x_in", [T, D], F32, kind="ExternalInput").ap()
    xT_in = nc.dram_tensor("xT_in", [D, T], F32, kind="ExternalInput").ap()
    wqt_in = nc.dram_tensor("wqt", [D, D], F32, kind="ExternalInput").ap()
    wkt_in = nc.dram_tensor("wkt", [D, D], F32, kind="ExternalInput").ap()
    wvt_in = nc.dram_tensor("wvt", [D, D], F32, kind="ExternalInput").ap()
    gwr_in = nc.dram_tensor("gwr", [128, 64], F32, kind="ExternalInput").ap()
    w1_in = nc.dram_tensor("w1", [D, FF], F32, kind="ExternalInput").ap()
    w2_in = nc.dram_tensor("w2", [FF, D], F32, kind="ExternalInput").ap()
    meta_in = nc.dram_tensor("meta", [128, 1], F32, kind="ExternalInput").ap()

    out_t = nc.dram_tensor("out", [T, D], F32, kind="ExternalOutput").ap()

    def scratch(name, shape, dtype):
        kind = "ExternalOutput" if DEBUG else "Internal"
        return nc.dram_tensor(name, shape, dtype, kind=kind).ap()

    eqp_d = scratch("eqp", [T, D], BF16)
    ek_d = scratch("ekd", [T, D], BF16)
    v_d = scratch("vd", [T, D], BF16)
    wt_d = scratch("wtd", [T, 256], BF16)
    att_d = scratch("attd", [T, D], BF16)
    y32_d = scratch("y32", [T, D], F32)
    ybf_d = scratch("ybf", [T, D], BF16)
    dbg_d = scratch("dbg", [128, 4096], F32)

    expin_d = scratch("expin", [NSLOT, D], BF16)
    rsout_d = scratch("rsout", [CAP, D], BF16)
    eout_d = scratch("eout", [CAP, D], BF16)
    agout_d = scratch("agout", [NSLOT + 128, D], BF16)
    cnt_in_d = nc.dram_tensor("cntin", [1, E], F32).ap()
    cnt_ag_d = nc.dram_tensor("cntag", [E, E], F32).ap()

    ctx = contextlib.ExitStack()

    def sbuf(name, shape, dtype):
        return ctx.enter_context(nc.sbuf_tensor(name, shape, dtype))

    def psum(name):
        return ctx.enter_context(nc.psum_tensor(name, [128, 512], F32))

    with ctx:
        dq_all = sbuf("dq_all", [128, NT * H], F32)
        pmax_all = sbuf("pmax_all", [128, NT], F32)
        slot_all = sbuf("slot_all", [128, NT], U32)
        lpos_all = sbuf("lpos_all", [128, NT], F32)
        oh_all = sbuf("oh_all", [128, NT * E], F32)
        iota8 = sbuf("iota8", [128, E], F32)
        iota64 = sbuf("iota64", [128, 64], F32)
        l2tri = sbuf("l2tri", [128, 128], F32)
        ones128 = sbuf("ones128", [128, 128], F32)
        maskB = sbuf("maskB", [8, 128], F32)
        b_sb = sbuf("b_sb", [128, E], F32)
        cnt_sb = sbuf("cnt_sb", [8, E], F32)
        meta_sb = sbuf("meta_sb", [128, 1], F32)
        gwc = sbuf("gwc", [128, 64], BF16)
        zer_bf = sbuf("zer_bf", [128, D], BF16)

        big_t = sbuf("big_t", [128, 1], F32)
        eps_t = sbuf("eps_t", [128, 1], F32)
        carry_sb = sbuf("carry_sb", [128, E], F32)
        et_all = sbuf("et_all", [128, NT], F32)

        ctx23 = contextlib.ExitStack()

        def sbuf23(name, shape, dtype):
            return ctx23.enter_context(nc.sbuf_tensor(name, shape, dtype))

        wqT = sbuf23("wqT", [128, 8 * D], BF16)
        wkT = sbuf23("wkT", [128, 8 * D], BF16)
        wvT = sbuf23("wvT", [128, 8 * D], BF16)

        xT_t = sbuf23("xT_t", [128, 2 * D], BF16)
        eq32 = sbuf23("eq32", [128, 2 * D], F32)
        ek32 = sbuf23("ek32", [128, 2 * D], F32)
        sd_t = sbuf23("sd_t", [128, 2 * HD], F32)
        eqp_t = sbuf23("eqp_t", [128, 2 * D], BF16)
        ekb_t = sbuf23("ekb_t", [128, 2 * D], BF16)
        vb_t = sbuf23("vb_t", [128, 2 * D], BF16)

        bd1 = sbuf23("bd1", [128, 2 * 2048], BF16)
        eq1 = sbuf23("eq1", [128, 2 * 1024], BF16)
        bd2 = sbuf23("bd2", [128, 2 * 2048], BF16)
        vs2 = sbuf23("vs2", [128, 2 * 1024], BF16)
        wsb = sbuf23("wsb", [128, 2 * 256], BF16)
        asb = sbuf23("asb", [128, 2 * 1024], BF16)


        PB = [psum(f"PB{i}") for i in range(8)]

        def sem(name):
            return ctx.enter_context(nc.semaphore(name))

        s_w0 = sem("s_w0")
        s_wq = sem("s_wq")
        s_ms = sem("s_ms")
        s_zin = sem("s_zin")
        s_xt = [sem("s_xt0"), sem("s_xt1")]
        s_pe2 = sem("s_pe2")
        s_ev2 = [sem(f"s_ev2_{i}") for i in range(4)]
        s_dve2 = sem("s_dve2")
        s_sto2 = [sem("s_sto20"), sem("s_sto21")]
        s_sto2g = [sem("s_sto2g0"), sem("s_sto2g1")]
        s_ld3 = [sem("s_ld30"), sem("s_ld31")]
        s_ld3b = [sem("s_ld3b0"), sem("s_ld3b1")]
        s_m1 = sem("s_m1")
        s_m1ev = sem("s_m1ev")
        s_m2 = sem("s_m2")
        s_m2ev = sem("s_m2ev")
        s_wtst = [sem("s_wtst0"), sem("s_wtst1")]
        s_attst = [sem("s_attst0"), sem("s_attst1")]
        s_ms3 = sem("s_ms3")
        s_ld4 = [sem("s_ld40"), sem("s_ld41")]
        s_yst = [sem("s_yst0"), sem("s_yst1")]
        s_yT = [sem("s_yT0"), sem("s_yT1")]
        s_va = sem("s_va")
        s_ssq = sem("s_ssq")
        s_ssq9 = sem("s_ssq9")
        s_rstd = sem("s_rstd")
        s_y = sem("s_y")
        s_gate = sem("s_gate")
        s_gev = sem("s_gev")
        s_lm = sem("s_lm")
        s_em = sem("s_em")
        s_ohg = sem("s_ohg")
        s_pa = sem("s_pa")
        s_5p = sem("s_5p")
        s_5r = sem("s_5r")
        s_cnt = sem("s_cnt")
        s_cc1 = sem("s_cc1")
        s_cc1b = sem("s_cc1b")
        s_bm = sem("s_bm")
        s_5b = sem("s_5b")
        s_ld6 = [sem("s_ld60"), sem("s_ld61")]
        s_sc6 = [sem("s_sc60"), sem("s_sc61")]
        s_cc2 = sem("s_cc2")
        s_w1l = sem("s_w1l")
        s_yTs = [sem("s_yTs0"), sem("s_yTs1")]
        s_mm1 = sem("s_mm1")
        s_h1 = sem("s_h1")
        s_mm2 = sem("s_mm2")
        s_eoev = sem("s_eoev")
        s_eost = [sem("s_eost0"), sem("s_eost1")]
        s_cc3 = sem("s_cc3")
        s_z9 = sem("s_z9")
        s_g9 = [sem("s_g90"), sem("s_g91")]
        s_ld9 = [sem("s_ld90"), sem("s_ld91")]
        s_va9 = sem("s_va9")
        s_rstd9 = sem("s_rstd9")
        s_o9 = sem("s_o9")
        s_ost9 = [sem("s_ost90"), sem("s_ost91")]
        s_dbg = sem("s_dbg")
        s_out = sem("s_out")
        s_chv = sem("s_chv")
        _chv = [0]
        s_cha = sem("s_cha")
        _cha = [0]

        def chain_v(v, inst):
            _chv[0] += 1
            inst.then_inc(s_chv, 1)
            v.wait_ge(s_chv, _chv[0])

        def chain_a(a, inst):
            _cha[0] += 1
            inst.then_inc(s_cha, 1)
            a.wait_ge(s_cha, _cha[0])

        with nc.Block("p0") as blk:

            @blk.gpsimd
            def _(g):
                g.iota(iota8[:, :], [[1, E]], base=0, channel_multiplier=0,
                       allow_small_or_imprecise_dtypes=True)
                g.iota(iota64[:, :], [[0, 8], [1, 8]], base=0,
                       channel_multiplier=0,
                       allow_small_or_imprecise_dtypes=True)
                g.iota(l2tri[:, :], [[1, 128]], base=0, channel_multiplier=-1,
                       allow_small_or_imprecise_dtypes=True)
                g.iota(maskB[0:8, :], [[0, 128]], base=0, channel_multiplier=1,
                       allow_small_or_imprecise_dtypes=True)
                g.memset(ones128[:, :], 1.0)
                g.memset(big_t[:, :], BIG)
                g.memset(eps_t[:, :], 1e-5)
                g.memset(carry_sb[:, :], 0.0)
                i = g.memset(zer_bf[:, :], 0.0)
                i.then_inc(s_ms, 1)
                i = g.dma_start(out=gwc[:, :], in_=gwr_in[:, :])
                i.then_inc(s_w0, 16)
                i = g.dma_start(out=meta_sb[:, :], in_=meta_in[:, :])
                i.then_inc(s_w0, 16)


        with nc.Block("p0b", no_gpsimd_drain=True) as blk:

            @blk.vector
            def _(v):
                v.tensor_single_scalar(l2tri[:, :], l2tri[:, :], 0.0, ALU.is_ge)
                v.wait_ge(s_w0, 32)
                v.tensor_scalar(maskB[0:8, :], maskB[0:8, :], meta_sb[0:8, 0:1],
                                None, ALU.is_lt)

        if BUILD_STAGE >= 2:
            DEPTH23 = 8
            _bu = [0, 0, 0]
            evac_cnt = []
            for _k in range(NT):
                row = []
                for _hi in range(6):
                    _b = (6 * _k + _hi) % 3
                    _bu[_b] += 1
                    row.append((_b, _bu[_b]))
                evac_cnt.append(row)

            with nc.Block("p23", no_gpsimd_drain=True) as blk:

                @blk.gpsimd
                def _(g):
                    for w_src, w_sb in ((wqt_in, wqT), (wkt_in, wkT),
                                        (wvt_in, wvT)):
                        for c in range(8):
                            i = g.dma_start(out=w_sb[:, D * c:D * (c + 1)],
                                            in_=w_src[128 * c:128 * (c + 1), :])
                            i.then_inc(s_wq, 16)
                    for k in range(NT + DEPTH23):
                        if k < NT:
                            if k >= 2:
                                g.wait_ge(s_pe2, 6 * (k - 1))
                            i = g.dma_start(
                                out=xT_t[:, D * (k % 2):D * (k % 2 + 1)].rearrange(
                                    "p (c t) -> p c t", c=8),
                                in_=AP(xT_in.tensor, 128 * k,
                                       [[T, 128], [128 * T, 8], [1, 128]]))
                            i.then_inc(s_xt[k % 2], 16)
                        if k >= 1 and k - 1 < NT:
                            j = k - 1
                            par = j % 2
                            o2 = D * par
                            g.wait_ge(s_dve2, j + 1)
                            for _hi in (4, 5):
                                _b, _c = evac_cnt[j][_hi]
                                g.wait_ge(s_ev2[_b], _c)
                            if j >= 2:
                                g.wait_ge(s_sto2g[par], 32 * (j // 2))
                            r0, r1 = 128 * j, 128 * (j + 1)
                            i = g.dma_start(out=ek_d[r0:r1, :],
                                            in_=ekb_t[:, o2:o2 + D])
                            i.then_inc(s_sto2g[par], 16)
                            i = g.dma_start(out=v_d[r0:r1, :],
                                            in_=vb_t[:, o2:o2 + D])
                            i.then_inc(s_sto2g[par], 16)
                        if k >= 4 and k - 4 < NT:
                            j = k - 4
                            par = j % 2
                            base = 128 * j
                            g.wait_ge(s_m1ev, j + 1)
                            if j >= 2:
                                g.wait_ge(s_wtst[par], 16 * (j // 2))
                            dst = AP(wt_d.tensor, base * 256,
                                     [[512, 4], [256, 2], [16, 16],
                                      [8 * 256, 16], [1, 16]])
                            i = g.dma_start(out=dst,
                                            in_=wsb[:, 256 * par:256 * (par + 1)]
                                            .rearrange("p (q h) -> p q h", h=16))
                            i.then_inc(s_wtst[par], 16)
                        if k >= 7 and k - 7 < NT:
                            j = k - 7
                            par = j % 2
                            g.wait_ge(s_m2ev, 2 * (j + 1))
                            if j >= 2:
                                g.wait_ge(s_attst[par], 32 * (j // 2))
                            for hb in range(2):
                                dst = AP(att_d.tensor, (128 * j + 64 * hb) * D,
                                         [[D, 8], [64, 16], [8 * D, 8], [1, 64]])
                                i = g.dma_start(
                                    out=dst,
                                    in_=asb[:, 1024 * par + 512 * hb:
                                            1024 * par + 512 * (hb + 1)]
                                    .rearrange("p (a e) -> p a e", e=64))
                                i.then_inc(s_attst[par], 16)

                @blk.sync
                def _(sy):
                    sy.wait_ge(s_ms3, 1)
                    for k in range(NT + DEPTH23):
                        if k >= 1 and k - 1 < NT:
                            j = k - 1
                            par = j % 2
                            o = D * par
                            sy.wait_ge(s_dve2, j + 1)
                            if j >= 2:
                                sy.wait_ge(s_sto2[par], 16 * (j // 2))
                            r0, r1 = 128 * j, 128 * (j + 1)
                            i = sy.dma_start(out=eqp_d[r0:r1, :],
                                             in_=eqp_t[:, o:o + D])
                            i.then_inc(s_sto2[par], 16)
                        if k >= 2 and k - 2 < NT:
                            j = k - 2
                            par = j % 2
                            base = 128 * j
                            sy.wait_ge(s_sto2[par], 16 * (j // 2 + 1))
                            sy.wait_ge(s_sto2g[par], 32 * (j // 2 + 1))
                            if j >= 2:
                                sy.wait_ge(s_m1, j - 1)
                            dst_e = bd1[0:64, 2048 * par:2048 * (par + 1)].rearrange(
                                "p (a c) -> p a c", c=32)[:, :, 0:16]
                            i = sy.dma_start(out=dst_e,
                                             in_=AP(ek_d.tensor, base * D,
                                                    [[16, 64], [2 * D, 64], [1, 16]]))
                            i.then_inc(s_ld3[par], 16)
                            dst_o = bd1[64:128, 2048 * par:2048 * (par + 1)].rearrange(
                                "p (a c) -> p a c", c=32)[:, :, 16:32]
                            i = sy.dma_start(out=dst_o,
                                             in_=AP(ek_d.tensor, (base + 1) * D,
                                                    [[16, 64], [2 * D, 64], [1, 16]]))
                            i.then_inc(s_ld3[par], 16)
                            i = sy.dma_start(
                                out=eq1[:, 1024 * par:1024 * (par + 1)].rearrange(
                                    "p (a h) -> p a h", h=16),
                                in_=AP(eqp_d.tensor, base * D,
                                       [[1024, 2], [16, 64], [2048, 64], [1, 16]]))
                            i.then_inc(s_ld3[par], 16)

                @blk.scalar
                def _(s):
                    s.wait_ge(s_ms3, 1)
                    for k in range(NT + DEPTH23):
                        if k < NT:
                            o = D * (k % 2)
                            if k >= 2:
                                s.wait_ge(s_dve2, k - 1)
                                s.wait_ge(s_sto2g[k % 2], 32 * (k // 2))
                            outs = (eq32[:, o:o + 512], eq32[:, o + 512:o + D],
                                    ek32[:, o:o + 512], ek32[:, o + 512:o + D],
                                    vb_t[:, o:o + 512], vb_t[:, o + 512:o + D])
                            for hi in range(6):
                                b, _c = evac_cnt[k][hi]
                                s.wait_ge(s_pe2, 6 * k + hi + 1)
                                if hi < 4:
                                    i = s.activation(outs[hi], PB[b][:, :], AF.Exp)
                                else:
                                    i = s.copy(outs[hi], PB[b][:, :])
                                i.then_inc(s_ev2[b], 1)
                        if k >= 5 and k - 5 < NT:
                            j = k - 5
                            par = j % 2
                            base = 128 * j
                            if j >= 2:
                                s.wait_ge(s_m2, j - 1)
                            s.wait_ge(s_wtst[par], 16 * (j // 2 + 1))
                            s.wait_ge(s_sto2g[par], 32 * (j // 2 + 1))
                            for j2 in range(8):
                                dst = bd2[16 * j2:16 * (j2 + 1),
                                          2048 * par:2048 * (par + 1)].rearrange(
                                    "p (a b) -> p a b", b=128)[:, :,
                                                              16 * j2:16 * (j2 + 1)]
                                i = s.dma_start(
                                    out=dst,
                                    in_=AP(wt_d.tensor, (base + j2) * 256,
                                           [[16, 16], [8 * 256, 16], [1, 16]]))
                                i.then_inc(s_ld3b[par], 16)
                            i = s.dma_start(
                                out=vs2[:, 1024 * par:1024 * (par + 1)].rearrange(
                                    "p (a e) -> p a e", e=64),
                                in_=AP(v_d.tensor, base * D,
                                       [[D, 8], [64, 16], [8 * D, 16], [1, 64]]))
                            i.then_inc(s_ld3b[par], 16)

                @blk.tensor
                def _(t):
                    t.wait_ge(s_wq, 24 * 16)
                    pe_bank_uses = [0, 0, 0]
                    for k in range(NT + DEPTH23):
                        if k < NT:
                            t.wait_ge(s_xt[k % 2], 16 * (k // 2 + 1))
                            for mi, wT in enumerate((wqT, wkT, wvT)):
                                for hf in range(2):
                                    hi = 2 * mi + hf
                                    b = (6 * k + hi) % 3
                                    if pe_bank_uses[b] > 0:
                                        t.wait_ge(s_ev2[b], pe_bank_uses[b])
                                    pe_bank_uses[b] += 1
                                    for c in range(8):
                                        lhsT = xT_t[:, D * (k % 2) + 128 * c:
                                                    D * (k % 2) + 128 * (c + 1)]
                                        i = t.matmul(PB[b][:, :], lhsT,
                                                     wT[:, D * c + 512 * hf:
                                                        D * c + 512 * (hf + 1)],
                                                     start=(c == 0), stop=(c == 7))
                                    i.then_inc(s_pe2, 1)
                        if k >= 3 and k - 3 < NT:
                            j = k - 3
                            par = j % 2
                            t.wait_ge(s_ld3[par], 48 * (j // 2 + 1))
                            if j >= 2:
                                t.wait_ge(s_m1ev, j - 1)
                            for jj in range(64):
                                c, q = jj % 4, jj // 4
                                i = t.matmul(
                                    PB[3 + par][32 * c:32 * c + 32,
                                                16 * q:16 * (q + 1)],
                                    bd1[:, 2048 * par + 32 * jj:
                                        2048 * par + 32 * (jj + 1)],
                                    eq1[:, 1024 * par + 16 * jj:
                                        1024 * par + 16 * (jj + 1)],
                                    start=True, stop=True,
                                    tile_position=(0, 32 * c))
                            i.then_inc(s_m1, 1)
                        if k >= 6 and k - 6 < NT:
                            j = k - 6
                            par = j % 2
                            t.wait_ge(s_ld3b[par], 144 * (j // 2 + 1))
                            t.wait_ge(s_m2ev, 2 * j)
                            for grp in range(16):
                                bank = PB[5 + grp // 8]
                                i = t.matmul(
                                    bank[:, 64 * (grp % 8):64 * (grp % 8 + 1)],
                                    bd2[:, 2048 * par + 128 * grp:
                                        2048 * par + 128 * (grp + 1)],
                                    vs2[:, 1024 * par + 64 * grp:
                                        1024 * par + 64 * (grp + 1)],
                                    start=True, stop=True)
                            i.then_inc(s_m2, 1)

                @blk.vector
                def _(v):
                    i = v.memset(bd1[:, :], 0.0)
                    i = v.memset(bd2[:, :], 0.0)
                    i.then_inc(s_ms3, 1)
                    for k in range(NT + DEPTH23):
                        if k < NT:
                            o = D * (k % 2)
                            oh = HD * (k % 2)
                            if k >= 2:
                                v.wait_ge(s_sto2[k % 2], 16 * (k // 2))
                                v.wait_ge(s_sto2g[k % 2], 32 * (k // 2))
                            for _hi in range(4):
                                _b, _c = evac_cnt[k][_hi]
                                v.wait_ge(s_ev2[_b], _c)
                            i = v.reduce_sum(sd_t[:, oh:oh + HD],
                                             ek32[:, o:o + D].rearrange(
                                                 "p (d h) -> p d h", h=H),
                                             axis=AX.X)
                            chain_v(v, i)
                            i = v.reciprocal(sd_t[:, oh:oh + HD],
                                             sd_t[:, oh:oh + HD])
                            chain_v(v, i)
                            v.tensor_tensor(
                                eqp_t[:, o:o + D].rearrange("p (d h) -> p d h",
                                                            h=H),
                                eq32[:, o:o + D].rearrange("p (d h) -> p d h",
                                                           h=H),
                                sd_t[:, oh:oh + HD].rearrange(
                                    "p (d u) -> p d u", u=1).broadcast_to(
                                    [128, HD, H]),
                                ALU.mult)
                            v.reduce_sum(dq_all[:, H * k:H * (k + 1)],
                                         eq32[:, o:o + D].rearrange(
                                             "p (d h) -> p h d", h=H),
                                         axis=AX.X)
                            i = v.tensor_copy(ekb_t[:, o:o + D],
                                              ek32[:, o:o + D])
                            i.then_inc(s_dve2, 1)
                        if k >= 3 and k - 3 < NT:
                            j = k - 3
                            par = j % 2
                            v.wait_ge(s_m1, j + 1)
                            if j >= 2:
                                v.wait_ge(s_wtst[par], 16 * (j // 2))
                            i = v.tensor_copy(wsb[:, 256 * par:256 * (par + 1)],
                                              PB[3 + par][:, 0:256])
                            i.then_inc(s_m1ev, 1)
                        if k >= 6 and k - 6 < NT:
                            j = k - 6
                            par = j % 2
                            v.wait_ge(s_m2, j + 1)
                            if j >= 2:
                                v.wait_ge(s_attst[par], 32 * (j // 2))
                            v.tensor_copy(asb[:, 1024 * par:1024 * par + 512],
                                          PB[5][:, :])
                            i = v.tensor_copy(asb[:, 1024 * par + 512:
                                                  1024 * par + 1024],
                                              PB[6][:, :])
                            i.then_inc(s_m2ev, 2)

        if BUILD_STAGE >= 4:
            ctx23.close()

            ctx45 = contextlib.ExitStack()

            def sbuf45(name, shape, dtype):
                return ctx45.enter_context(nc.sbuf_tensor(name, shape, dtype))

            xf_t = sbuf45("xf_t", [128, 2 * D], F32)
            ab_t = sbuf45("ab_t", [128, 2 * D], BF16)
            s1_t = sbuf45("s1_t", [128, 2 * D], F32)
            y32t = sbuf45("y32t", [128, 2 * D], F32)
            ybf_t = sbuf45("ybf_t", [128, 2 * D], BF16)
            yT_t = sbuf45("yT_t", [128, 2 * 8 * D], BF16)
            lgb = sbuf45("lgb", [128, 2 * 64], F32)
            lmb = sbuf45("lmb", [128, 2 * 64], F32)
            exb = sbuf45("exb", [128, 2 * 64], F32)
            c64 = sbuf45("c64", [128, 2 * 64], F32)
            em_t = sbuf45("em_t", [128, 2 * 8], F32)
            cum_t = sbuf45("cum_t", [128, 2 * 8], F32)
            ex8_t = sbuf45("ex8_t", [128, 2 * 8], F32)
            sc4 = sbuf45("sc4", [128, 32], F32)
            sq_t = sbuf45("sq_t", [128, D], F32)

            with nc.Block("p45", no_gpsimd_drain=True) as blk:

                @blk.sync
                def _(sy):
                    sy.wait_ge(s_attst[0], 32 * 16)
                    sy.wait_ge(s_attst[1], 32 * 16)
                    for k in range(NT + 2):
                        par = k % 2
                        if k < NT:
                            if k >= 2:
                                sy.wait_ge(s_y, k - 1)
                            i = sy.dma_start(out=xf_t[:, D * par:D * (par + 1)],
                                             in_=x_in[128 * k:128 * (k + 1), :])
                            i.then_inc(s_ld4[par], 16)
                            i = sy.dma_start(out=ab_t[:, D * par:D * (par + 1)],
                                             in_=att_d[128 * k:128 * (k + 1), :])
                            i.then_inc(s_ld4[par], 16)
                        if k >= 1 and k - 1 < NT:
                            kp = k - 1
                            pp = kp % 2
                            sy.wait_ge(s_y, kp + 1)
                            if kp >= 2:
                                sy.wait_ge(s_yst[pp], 32 * (kp // 2))
                            i = sy.dma_start(out=y32_d[128 * kp:128 * (kp + 1), :],
                                             in_=y32t[:, D * pp:D * (pp + 1)])
                            i.then_inc(s_yst[pp], 16)
                            i = sy.dma_start(out=ybf_d[128 * kp:128 * (kp + 1), :],
                                             in_=ybf_t[:, D * pp:D * (pp + 1)])
                            i.then_inc(s_yst[pp], 16)
                        if k >= 8 and (k - 8) % 8 == 0:
                            G = (k - 8) // 8
                            gp_ = G % 2
                            sy.wait_ge(s_yst[0], 32 * 4 * (G + 1))
                            sy.wait_ge(s_yst[1], 32 * 4 * (G + 1))
                            if G >= 2:
                                sy.wait_ge(s_gate, 8 * (G - 1))
                            for c in range(8):
                                i = sy.dma_start(
                                    out=yT_t[:, 8 * D * gp_ + 1024 * c:
                                             8 * D * gp_ + 1024 * (c + 1)],
                                    in_=ybf_d[1024 * G:1024 * (G + 1),
                                              128 * c:128 * (c + 1)],
                                    transpose=True)
                                i.then_inc(s_yT[gp_], 16)

                @blk.vector
                def _(v):
                    for k in range(NT + 10):
                        par = k % 2
                        o = D * par
                        so = 16 * par
                        if k < NT:
                            v.wait_ge(s_ld4[par], 32 * (k // 2 + 1))
                            if k >= 2:
                                v.wait_ge(s_yst[par], 32 * (k // 2))
                                v.wait_ge(s_rstd, k - 1)
                            i = v.reciprocal(sc4[:, so:so + 16],
                                             dq_all[:, H * k:H * (k + 1)])
                            chain_v(v, i)
                            i = v.tensor_tensor(
                                s1_t[:, o:o + D].rearrange("p (h e) -> p h e", e=HD),
                                ab_t[:, o:o + D].rearrange("p (h e) -> p h e", e=HD),
                                sc4[:, so:so + 16].rearrange(
                                    "p (h u) -> p h u", u=1).broadcast_to(
                                    [128, H, HD]),
                                ALU.mult)
                            chain_v(v, i)
                            i = v.tensor_add(s1_t[:, o:o + D], s1_t[:, o:o + D],
                                             xf_t[:, o:o + D])
                            i.then_inc(s_va, 1)
                            v.wait_ge(s_va, 3 * k + 1)
                            i = v.reduce_sum(sc4[:, so + 8:so + 9],
                                            s1_t[:, o:o + D], axis=AX.X)
                            chain_v(v, i)
                            i = v.tensor_scalar_mul(sc4[:, so + 9:so + 10],
                                                    sc4[:, so + 8:so + 9],
                                                    1.0 / D)
                            i.then_inc(s_va, 1)
                            v.wait_ge(s_va, 3 * k + 2)
                            i = v.tensor_mul(sc4[:, so + 13:so + 14],
                                             sc4[:, so + 9:so + 10],
                                             sc4[:, so + 9:so + 10])
                            chain_v(v, i)
                            v.wait_ge(s_ssq, k + 1)
                            i = v.scalar_tensor_tensor(sc4[:, so + 14:so + 15],
                                                       sc4[:, so + 15:so + 16],
                                                       1.0 / D,
                                                       sc4[:, so + 13:so + 14],
                                                       ALU.mult, ALU.subtract)
                            i.then_inc(s_va, 1)
                            v.wait_ge(s_rstd, k + 1)
                            i = v.scalar_tensor_tensor(
                                y32t[:, o:o + D], s1_t[:, o:o + D],
                                sc4[:, so + 9:so + 10],
                                sc4[:, so + 10:so + 11].broadcast_to([128, D]),
                                ALU.subtract, ALU.mult)
                            chain_v(v, i)
                            i = v.tensor_copy(ybf_t[:, o:o + D], y32t[:, o:o + D])
                            i.then_inc(s_y, 1)
                        if k >= 10 and k - 10 < NT:
                            kk = k - 10
                            pp = kk % 2
                            if kk % 8 == 0:
                                G = kk // 8
                                gp_ = G % 2
                                go = 64 * gp_
                                v.wait_ge(s_gate, 8 * (G + 1))
                                if G >= 2:
                                    v.wait_ge(s_em, G - 1)
                                    v.wait_ge(s_pa, G - 1)
                                i = v.tensor_copy(lgb[:, go:go + 64],
                                                  PB[6 + gp_][:, 0:64])
                                i.then_inc(s_gev, 1)
                                v.wait_ge(s_gev, G + 1)
                                i = v.reduce_max(em_t[:, 8 * gp_:8 * (gp_ + 1)],
                                                 lgb[:, go:go + 64].rearrange(
                                                     "p (g e) -> p g e", e=8),
                                                 axis=AX.X)
                                chain_v(v, i)
                                m8b = em_t[:, 8 * gp_:8 * (gp_ + 1)].rearrange(
                                    "p (g u) -> p g u", u=1).broadcast_to(
                                    [128, 8, 8])
                                i = v.tensor_tensor(
                                    lmb[:, go:go + 64].rearrange(
                                        "p (g e) -> p g e", e=8),
                                    lgb[:, go:go + 64].rearrange(
                                        "p (g e) -> p g e", e=8),
                                    m8b, ALU.subtract)
                                i.then_inc(s_lm, 1)
                                i = v.tensor_tensor(
                                    c64[:, go:go + 64].rearrange(
                                        "p (g e) -> p g e", e=8),
                                    lgb[:, go:go + 64].rearrange(
                                        "p (g e) -> p g e", e=8),
                                    m8b, ALU.is_equal)
                                chain_v(v, i)
                                i = v.tensor_scalar(c64[:, go:go + 64],
                                                    c64[:, go:go + 64],
                                                    -BIG, BIG,
                                                    ALU.mult, ALU.add)
                                chain_v(v, i)
                                i = v.tensor_add(c64[:, go:go + 64],
                                                 c64[:, go:go + 64],
                                                 iota64[:, :])
                                chain_v(v, i)
                                i = v.tensor_reduce(
                                    et_all[:, 8 * G:8 * (G + 1)],
                                    c64[:, go:go + 64].rearrange(
                                        "p (g e) -> p g e", e=8),
                                    AX.X, ALU.min)
                                chain_v(v, i)
                                etb = et_all[:, 8 * G:8 * (G + 1)].rearrange(
                                    "p (g u) -> p g u", u=1).broadcast_to(
                                    [128, 8, 8])
                                i = v.tensor_tensor(
                                    oh_all[:, 64 * G:64 * (G + 1)].rearrange(
                                        "p (g e) -> p g e", e=8),
                                    iota64[:, :].rearrange(
                                        "p (g e) -> p g e", e=8),
                                    etb, ALU.is_equal)
                                i.then_inc(s_ohg, 1)
                                v.wait_ge(s_em, G + 1)
                                i = v.reduce_sum(cum_t[:, 8 * gp_:8 * (gp_ + 1)],
                                                 exb[:, go:go + 64].rearrange(
                                                     "p (g e) -> p g e", e=8),
                                                 axis=AX.X)
                                chain_v(v, i)
                                i = v.reciprocal(pmax_all[:, 8 * G:8 * (G + 1)],
                                                 cum_t[:, 8 * gp_:8 * (gp_ + 1)])
                                i.then_inc(s_pa, 1)
                            v.wait_ge(s_5p, kk + 1)
                            i = v.tensor_mul(ex8_t[:, 8 * pp:8 * (pp + 1)],
                                             oh_all[:, 8 * kk:8 * (kk + 1)],
                                             PB[4 + pp][:, 0:8])
                            chain_v(v, i)
                            i = v.reduce_sum(lpos_all[:, kk:kk + 1],
                                             ex8_t[:, 8 * pp:8 * (pp + 1)],
                                             axis=AX.X)
                            chain_v(v, i)
                            i = v.tensor_scalar_add(lpos_all[:, kk:kk + 1],
                                                    lpos_all[:, kk:kk + 1], -1.0)
                            chain_v(v, i)
                            i = v.tensor_add(carry_sb[:, :], carry_sb[:, :],
                                             PB[4 + pp][:, 8:16])
                            i.then_inc(s_5r, 1)
                    v.wait_ge(s_5r, NT)
                    i = v.tensor_copy(cnt_sb[0:1, 0:8], carry_sb[0:1, 0:8])
                    i.then_inc(s_cnt, 1)

                @blk.gpsimd
                def _(g):
                    g.wait_ge(s_ms, 1)
                    for r in range(NSLOT // 128):
                        i = g.dma_start(out=expin_d[128 * r:128 * (r + 1), :],
                                        in_=zer_bf[:, :])
                        i.then_inc(s_zin, 16)
                    i = g.dma_start(out=agout_d[NSLOT:NSLOT + 128, :],
                                    in_=zer_bf[:, :])
                    i.then_inc(s_zin, 16)

                @blk.scalar
                def _(s):
                    for k in range(NT + 10):
                        par = k % 2
                        o = D * par
                        so = 16 * par
                        if k < NT:
                            s.wait_ge(s_va, 3 * k + 1)
                            i = s.activation(sq_t[:, :], s1_t[:, o:o + D],
                                             AF.Square,
                                             accum_out=sc4[:, so + 15:so + 16])
                            i.then_inc(s_ssq, 1)
                            s.wait_ge(s_va, 3 * k + 3)
                            i = s.activation(sc4[:, so + 14:so + 15],
                                             sc4[:, so + 14:so + 15], AF.Ln,
                                             bias=eps_t[:, :])
                            chain_a(s, i)
                            i = s.activation(sc4[:, so + 10:so + 11],
                                             sc4[:, so + 14:so + 15], AF.Exp,
                                             scale=-0.5)
                            i.then_inc(s_rstd, 1)
                        if k >= 10 and k - 10 < NT and (k - 10) % 8 == 0:
                            G = (k - 10) // 8
                            gp_ = G % 2
                            go = 64 * gp_
                            s.wait_ge(s_lm, G + 1)
                            if G >= 2:
                                s.wait_ge(s_pa, G - 1)
                            i = s.activation(exb[:, go:go + 64],
                                             lmb[:, go:go + 64], AF.Exp)
                            i.then_inc(s_em, 1)

                @blk.tensor
                def _(t):
                    for k in range(NT + 10):
                        if k >= 9 and (k - 9) % 8 == 0 and (k - 9) // 8 < NT // 8:
                            G = (k - 9) // 8
                            gp_ = G % 2
                            t.wait_ge(s_yT[gp_], 8 * 16 * (G // 2 + 1))
                            if G >= 2:
                                t.wait_ge(s_gev, G - 1)
                            for j in range(8):
                                kk = 8 * G + j
                                for c in range(8):
                                    i = t.matmul(
                                        PB[6 + gp_][:, 8 * j:8 * (j + 1)],
                                        yT_t[:, 8 * D * gp_ + 1024 * c + 128 * j:
                                             8 * D * gp_ + 1024 * c + 128 * (j + 1)],
                                        gwc[:, 8 * c:8 * (c + 1)],
                                        start=(c == 0), stop=(c == 7))
                                i.then_inc(s_gate, 1)
                            t.wait_ge(s_ohg, G + 1)
                            for j in range(8):
                                kk = 8 * G + j
                                pp = kk % 2
                                if kk >= 2:
                                    t.wait_ge(s_5r, kk - 1)
                                t.matmul(PB[4 + pp][:, 0:8], l2tri[:, :],
                                         oh_all[:, 8 * kk:8 * (kk + 1)],
                                         start=True, stop=False)
                                t.wait_ge(s_5r, kk)
                                t.matmul(PB[4 + pp][:, 0:8], ones128[0:1, :],
                                         carry_sb[0:1, 0:8],
                                         start=False, stop=True)
                                i = t.matmul(PB[4 + pp][:, 8:16], ones128[:, :],
                                             oh_all[:, 8 * kk:8 * (kk + 1)],
                                             start=True, stop=True)
                                i.then_inc(s_5p, 1)

        if BUILD_STAGE >= 4:
            ctx45.close()

            ctx67 = contextlib.ExitStack()

            def sbuf67(name, shape, dtype):
                return ctx67.enter_context(nc.sbuf_tensor(name, shape, dtype))

            em5 = sbuf67("em5", [128, 8], F32)
            sc5 = sbuf67("sc5", [128, 8], F32)
            ysc_t = sbuf67("ysc_t", [128, 2 * D], BF16)
            if BUILD_STAGE >= 9:
                w1big = sbuf67("w1big", [128, 8 * FF], BF16)
                w2big = sbuf67("w2big", [128, 32 * D], BF16)
                yTs = sbuf67("yTs", [128, 2 * 8 * 512], BF16)
                hT = sbuf67("hT", [128, 32 * 512], BF16)
                eo_t = sbuf67("eo_t", [128, 2 * D], BF16)

            with nc.Block("p5c", no_gpsimd_drain=True) as blk:

                @blk.gpsimd
                def _(g):
                    i = g.dma_start(out=cnt_in_d[:, :], in_=cnt_sb[0:1, 0:8])
                    i.then_inc(s_cc1, 16)
                    g.wait_ge(s_cc1, 16)
                    i = g.collective_compute(
                        "AllGather", ALU.bypass,
                        replica_groups=[list(range(NCORES))],
                        ins=[cnt_in_d], outs=[cnt_ag_d])
                    i.then_inc(s_cc1b, 1)
                    g.wait_ge(s_cc1b, 1)
                    i = g.dma_start(out=cnt_sb[0:8, 0:8], in_=cnt_ag_d[:, :])
                    i.then_inc(s_cc1, 16)

                @blk.tensor
                def _(t):
                    t.wait_ge(s_cc1, 32)
                    i = t.matmul(PB[4][:, 0:8], maskB[0:8, :], cnt_sb[0:8, 0:8],
                                 start=True, stop=True)
                    i.then_inc(s_bm, 1)

                @blk.vector
                def _(v):
                    v.wait_ge(s_bm, 1)
                    v.tensor_copy(b_sb[:, :], PB[4][:, 0:8])

            with nc.Block("p5d", no_gpsimd_drain=True) as blk:

                @blk.vector
                def _(v):
                    for kk in range(NT):
                        i = v.tensor_mul(em5[:, 0:8], oh_all[:, 8 * kk:8 * (kk + 1)],
                                         b_sb[:, :])
                        chain_v(v, i)
                        i = v.reduce_sum(sc5[:, 0:1], em5[:, 0:8], axis=AX.X)
                        chain_v(v, i)
                        i = v.tensor_add(sc5[:, 0:1], sc5[:, 0:1],
                                         lpos_all[:, kk:kk + 1])
                        chain_v(v, i)
                        i = v.tensor_single_scalar(sc5[:, 1:2], sc5[:, 0:1],
                                                   float(CAP), ALU.is_ge)
                        chain_v(v, i)
                        i = v.scalar_tensor_tensor(sc5[:, 2:3],
                                                   et_all[:, kk:kk + 1],
                                                   float(CAP),
                                                   sc5[:, 0:1],
                                                   ALU.mult, ALU.add)
                        chain_v(v, i)
                        i = v.scalar_tensor_tensor(sc5[:, 3:4],
                                                   sc5[:, 1:2], BIG,
                                                   sc5[:, 2:3],
                                                   ALU.mult, ALU.add)
                        chain_v(v, i)
                        i = v.tensor_single_scalar(sc5[:, 3:4], sc5[:, 3:4],
                                                   float(NSLOT), ALU.min)
                        chain_v(v, i)
                        i = v.tensor_copy(slot_all[:, kk:kk + 1], sc5[:, 3:4])
                        i.then_inc(s_5b, 1)

                @blk.sync
                def _(sy):
                    for kk in range(NT):
                        par = kk % 2
                        if kk >= 2:
                            sy.wait_ge(s_sc6[par], 16 * (kk // 2))
                        i = sy.dma_start(out=ysc_t[:, D * par:D * (par + 1)],
                                         in_=ybf_d[128 * kk:128 * (kk + 1), :])
                        i.then_inc(s_ld6[par], 16)

                @blk.gpsimd
                def _(g):
                    g.wait_ge(s_zin, 16 * (NSLOT // 128 + 1))
                    for kk in range(NT):
                        par = kk % 2
                        g.wait_ge(s_5b, kk + 1)
                        g.wait_ge(s_ld6[par], 16 * (kk // 2 + 1))
                        i = g.indirect_dma_start(
                            out=expin_d[:, :],
                            out_offset=bass.IndirectOffsetOnAxis(
                                ap=slot_all[:, kk:kk + 1], axis=0),
                            in_=ysc_t[:, D * par:D * (par + 1)],
                            in_offset=None,
                            bounds_check=NSLOT - 1, oob_is_err=False)
                        i.then_inc(s_sc6[par], 16)
                    if BUILD_STAGE >= 9:
                        for c in range(8):
                            i = g.dma_start(out=w1big[:, FF * c:FF * (c + 1)],
                                            in_=w1_in[128 * c:128 * (c + 1), :])
                            i.then_inc(s_w1l, 16)
                        for f in range(32):
                            i = g.dma_start(out=w2big[:, D * f:D * (f + 1)],
                                            in_=w2_in[128 * f:128 * (f + 1), :])
                            i.then_inc(s_w1l, 16)
                    g.wait_ge(s_sc6[0], 16 * (NT // 2))
                    g.wait_ge(s_sc6[1], 16 * (NT // 2))
                    i = g.collective_compute(
                        "ReduceScatter", ALU.add,
                        replica_groups=[list(range(NCORES))],
                        ins=[expin_d], outs=[rsout_d])
                    i.then_inc(s_cc2, 1)

        if BUILD_STAGE >= 9:
            with nc.Block("p7", no_gpsimd_drain=True) as blk:
                MSZ = [512, 512, 256]

                @blk.sync
                def _(sy):
                    sy.wait_ge(s_cc2, 1)
                    for m in range(3):
                        msz = MSZ[m]
                        par = m % 2
                        if m >= 2:
                            sy.wait_ge(s_mm1, 32 * (m - 1))
                        for c in range(8):
                            i = sy.dma_start(
                                out=yTs[:, 4096 * par + 512 * c:
                                        4096 * par + 512 * c + msz],
                                in_=rsout_d[512 * m:512 * m + msz,
                                            128 * c:128 * (c + 1)],
                                transpose=True)
                            i.then_inc(s_yTs[par], 16)

                @blk.tensor
                def _(t):
                    t.wait_ge(s_w1l, 40 * 16)
                    stg = 0
                    for m in range(3):
                        msz = MSZ[m]
                        par = m % 2
                        t.wait_ge(s_yTs[par], 8 * 16 * (m // 2 + 1))
                        for f in range(32):
                            fi = 32 * m + f
                            if fi >= 2:
                                t.wait_ge(s_h1, fi - 1)
                            for d_ in range(8):
                                i = t.matmul(
                                    PB[f % 2][:, 0:msz],
                                    w1big[:, FF * d_ + 128 * f:
                                          FF * d_ + 128 * (f + 1)],
                                    yTs[:, 4096 * par + 512 * d_:
                                        4096 * par + 512 * d_ + msz],
                                    start=(d_ == 0), stop=(d_ == 7))
                            i.then_inc(s_mm1, 1)
                        t.wait_ge(s_h1, 32 * (m + 1))
                        for st in range(msz // 128):
                            pp2 = stg % 2
                            bk = (PB[2 + 2 * pp2], PB[3 + 2 * pp2])
                            if stg >= 2:
                                t.wait_ge(s_eoev, 2 * (stg - 1))
                            for f in range(32):
                                for hf in range(2):
                                    i = t.matmul(
                                        bk[hf][:, :],
                                        hT[:, 512 * f + 128 * st:
                                           512 * f + 128 * (st + 1)],
                                        w2big[:, D * f + 512 * hf:
                                              D * f + 512 * (hf + 1)],
                                        start=(f == 0), stop=(f == 31))
                            i.then_inc(s_mm2, 1)
                            stg += 1

                @blk.scalar
                def _(s):
                    for m in range(3):
                        msz = MSZ[m]
                        for f in range(32):
                            fi = 32 * m + f
                            s.wait_ge(s_mm1, fi + 1)
                            i = s.activation(hT[:, 512 * f:512 * f + msz],
                                             PB[f % 2][:, 0:msz], AF.Relu)
                            i.then_inc(s_h1, 1)

                @blk.vector
                def _(v):
                    for stg in range(10):
                        pp2 = stg % 2
                        v.wait_ge(s_mm2, stg + 1)
                        if stg >= 2:
                            v.wait_ge(s_eost[pp2], 16 * (stg // 2))
                        v.tensor_copy(eo_t[:, D * pp2:D * pp2 + 512],
                                      PB[2 + 2 * pp2][:, :])
                        i = v.tensor_copy(eo_t[:, D * pp2 + 512:D * (pp2 + 1)],
                                          PB[3 + 2 * pp2][:, :])
                        i.then_inc(s_eoev, 2)

                @blk.gpsimd
                def _(g):
                    for stg in range(10):
                        pp2 = stg % 2
                        g.wait_ge(s_eoev, 2 * (stg + 1))
                        if stg >= 2:
                            g.wait_ge(s_eost[pp2], 16 * (stg // 2))
                        i = g.dma_start(out=eout_d[128 * stg:128 * (stg + 1), :],
                                        in_=eo_t[:, D * pp2:D * (pp2 + 1)])
                        i.then_inc(s_eost[pp2], 16)
                    g.wait_ge(s_eost[0], 16 * 5)
                    g.wait_ge(s_eost[1], 16 * 5)
                    i = g.collective_compute(
                        "AllGather", ALU.bypass,
                        replica_groups=[list(range(NCORES))],
                        ins=[eout_d], outs=[agout_d[0:NSLOT, :]])
                    i.then_inc(s_cc3, 1)

            ctx67.close()

            ctx9 = contextlib.ExitStack()

            def sbuf9(name, shape, dtype):
                return ctx9.enter_context(nc.sbuf_tensor(name, shape, dtype))

            gat_t = sbuf9("gat_t", [128, 2 * D], BF16)
            yf9_t = sbuf9("yf9_t", [128, 2 * D], F32)
            o9_t = sbuf9("o9_t", [128, 2 * D], F32)
            sq9 = sbuf9("sq9", [128, D], F32)
            sc9 = sbuf9("sc9", [128, 32], F32)

            with nc.Block("p9", no_gpsimd_drain=True) as blk:

                @blk.gpsimd
                def _(g):
                    g.wait_ge(s_cc3, 1)
                    for kk in range(NT):
                        par = kk % 2
                        if kk >= 2:
                            g.wait_ge(s_o9, kk - 1)
                        i = g.indirect_dma_start(
                            out=gat_t[:, D * par:D * (par + 1)],
                            out_offset=None,
                            in_=agout_d[:, :],
                            in_offset=bass.IndirectOffsetOnAxis(
                                ap=slot_all[:, kk:kk + 1], axis=0))
                        i.then_inc(s_g9[par], 16)

                @blk.sync
                def _(sy):
                    for kk in range(NT + 1):
                        par = kk % 2
                        if kk < NT:
                            if kk >= 2:
                                sy.wait_ge(s_o9, kk - 1)
                            i = sy.dma_start(
                                out=yf9_t[:, D * par:D * (par + 1)],
                                in_=y32_d[128 * kk:128 * (kk + 1), :])
                            i.then_inc(s_ld9[par], 16)
                        if kk >= 1:
                            kp = kk - 1
                            pp = kp % 2
                            sy.wait_ge(s_o9, kp + 1)
                            if kp >= 2:
                                sy.wait_ge(s_ost9[pp], 16 * (kp // 2))
                            i = sy.dma_start(
                                out=out_t[128 * kp:128 * (kp + 1), :],
                                in_=o9_t[:, D * pp:D * (pp + 1)])
                            i.then_inc(s_ost9[pp], 16)

                @blk.vector
                def _(v):
                    for kk in range(NT):
                        par = kk % 2
                        so = 16 * par
                        v.wait_ge(s_g9[par], 16 * (kk // 2 + 1))
                        v.wait_ge(s_ld9[par], 16 * (kk // 2 + 1))
                        if kk >= 2:
                            v.wait_ge(s_ost9[par], 16 * (kk // 2))
                            v.wait_ge(s_rstd9, kk - 1)
                        i = v.tensor_scalar(o9_t[:, D * par:D * (par + 1)],
                                            gat_t[:, D * par:D * (par + 1)],
                                            pmax_all[:, kk:kk + 1], None,
                                            ALU.mult)
                        chain_v(v, i)
                        i = v.tensor_add(o9_t[:, D * par:D * (par + 1)],
                                         o9_t[:, D * par:D * (par + 1)],
                                         yf9_t[:, D * par:D * (par + 1)])
                        i.then_inc(s_va9, 1)
                        v.wait_ge(s_va9, 3 * kk + 1)
                        i = v.reduce_sum(sc9[:, so:so + 1],
                                         o9_t[:, D * par:D * (par + 1)],
                                         axis=AX.X)
                        chain_v(v, i)
                        i = v.tensor_scalar_mul(sc9[:, so + 1:so + 2],
                                                sc9[:, so:so + 1], 1.0 / D)
                        i.then_inc(s_va9, 1)
                        v.wait_ge(s_va9, 3 * kk + 2)
                        i = v.tensor_mul(sc9[:, so + 5:so + 6],
                                         sc9[:, so + 1:so + 2],
                                         sc9[:, so + 1:so + 2])
                        chain_v(v, i)
                        v.wait_ge(s_ssq9, kk + 1)
                        i = v.scalar_tensor_tensor(sc9[:, so + 2:so + 3],
                                                   sc9[:, so + 6:so + 7],
                                                   1.0 / D,
                                                   sc9[:, so + 5:so + 6],
                                                   ALU.mult, ALU.subtract)
                        i.then_inc(s_va9, 1)
                        v.wait_ge(s_rstd9, kk + 1)
                        i = v.scalar_tensor_tensor(
                            o9_t[:, D * par:D * (par + 1)],
                            o9_t[:, D * par:D * (par + 1)],
                            sc9[:, so + 1:so + 2],
                            sc9[:, so + 4:so + 5].broadcast_to([128, D]),
                            ALU.subtract, ALU.mult)
                        i.then_inc(s_o9, 1)

                @blk.scalar
                def _(s):
                    for kk in range(NT):
                        par = kk % 2
                        so = 16 * par
                        s.wait_ge(s_va9, 3 * kk + 1)
                        i = s.activation(sq9[:, :], o9_t[:, D * par:D * (par + 1)],
                                         AF.Square,
                                         accum_out=sc9[:, so + 6:so + 7])
                        i.then_inc(s_ssq9, 1)
                        s.wait_ge(s_va9, 3 * kk + 3)
                        i = s.activation(sc9[:, so + 2:so + 3],
                                         sc9[:, so + 2:so + 3], AF.Ln,
                                         bias=eps_t[:, :])
                        chain_a(s, i)
                        i = s.activation(sc9[:, so + 4:so + 5],
                                         sc9[:, so + 2:so + 3], AF.Exp,
                                         scale=-0.5)
                        i.then_inc(s_rstd9, 1)

            ctx9.close()

        if BUILD_STAGE < 4:
            ctx23.close()
        with nc.Block("pend", no_gpsimd_drain=True) as blk:

            @blk.gpsimd
            def _(g):
                g.wait_ge(s_sto2[0], 16 * (NT // 2))
                g.wait_ge(s_sto2[1], 16 * (NT // 2))
                g.wait_ge(s_sto2g[0], 32 * (NT // 2))
                g.wait_ge(s_sto2g[1], 32 * (NT // 2))
                if BUILD_STAGE >= 3:
                    g.wait_ge(s_attst[0], 32 * 16)
                    g.wait_ge(s_attst[1], 32 * 16)
                if BUILD_STAGE >= 4:
                    g.wait_ge(s_yst[0], 32 * 16)
                    g.wait_ge(s_yst[1], 32 * 16)
                if BUILD_STAGE >= 9:
                    g.wait_ge(s_ost9[0], 16 * (NT // 2))
                    g.wait_ge(s_ost9[1], 16 * (NT // 2))
                    if DEBUG:
                        i = g.dma_start(out=dbg_d[:, 512:544], in_=pmax_all[:, :])
                        i.then_inc(s_dbg, 16)
                        i = g.dma_start(out=dbg_d[:, 544:576], in_=lpos_all[:, :])
                        i.then_inc(s_dbg, 16)
                        i = g.dma_start(out=dbg_d[:, 576:608], in_=et_all[:, :])
                        i.then_inc(s_dbg, 16)
                        i = g.dma_start(out=dbg_d[:, 608:640], in_=slot_all[:, :])
                        i.then_inc(s_dbg, 16)
                        i = g.dma_start(out=dbg_d[:, 640:648], in_=b_sb[:, :])
                        i.then_inc(s_dbg, 16)
                        i = g.dma_start(out=dbg_d[0:8, 648:656],
                                        in_=cnt_sb[0:8, 0:8])
                        i.then_inc(s_dbg, 16)
                if DEBUG:
                    i = g.dma_start(out=dbg_d[:, 0:NT * H], in_=dq_all[:, :])
                    i.then_inc(s_dbg, 16)
                if BUILD_STAGE < 9:
                    for k in range(NT):
                        i = g.dma_start(
                            out=out_t[128 * k:128 * (k + 1), 0:D],
                            in_=zer_bf[:, :])
                        i.then_inc(s_out, 16)
                g.wait_ge(s_out, 16 * NT if BUILD_STAGE < 9 else 0)
                if DEBUG:
                    g.wait_ge(s_dbg, 16 * (7 if BUILD_STAGE >= 4 else 1))

    return nc


def _stage_inputs(L0_attn_input_seq_embs, L0_attn_W_Q, L0_attn_W_K,
                  L0_attn_W_V, L0_ffn_gate_w, L0_ffn_w1, L0_ffn_w2):
    x = np.ascontiguousarray(L0_attn_input_seq_embs.reshape(32768, 1024))
    xT = np.ascontiguousarray(x.T)
    op = np.arange(1024)
    perm = (op % 16) * 64 + op // 16
    wqt = np.ascontiguousarray(L0_attn_W_Q.T[:, perm])
    wkt = np.ascontiguousarray(L0_attn_W_K.T[:, perm])
    wvt = np.ascontiguousarray(L0_attn_W_V.T)
    gwr = np.ascontiguousarray(
        L0_ffn_gate_w.reshape(8, 128, 8).transpose(1, 0, 2).reshape(128, 64))
    in_maps = []
    for i in range(NCORES):
        in_maps.append({
            "x_in": x[T * i:T * (i + 1)],
            "xT_in": np.ascontiguousarray(xT[:, T * i:T * (i + 1)]),
            "wqt": wqt, "wkt": wkt, "wvt": wvt, "gwr": gwr,
            "w1": np.ascontiguousarray(L0_ffn_w1[i]),
            "w2": np.ascontiguousarray(L0_ffn_w2[i]),
            "meta": np.full((128, 1), float(i), dtype=np.float32),
        })
    return in_maps


def kernel(L0_attn_input_seq_embs, L0_attn_W_Q, L0_attn_W_K, L0_attn_W_V,
           L0_ffn_gate_w, L0_ffn_w1, L0_ffn_w2):
    in_maps = _stage_inputs(L0_attn_input_seq_embs, L0_attn_W_Q, L0_attn_W_K,
                            L0_attn_W_V, L0_ffn_gate_w, L0_ffn_w1, L0_ffn_w2)
    nc = build()
    res = run_bass_kernel_spmd(nc, in_maps, core_ids=list(range(NCORES)))
    out = np.concatenate([r["out"] for r in res.results], axis=0)
    return out.reshape(2, 16384, 1024)


# version 10
# speedup vs baseline: 1.0089x; 1.0089x over previous
import contextlib
import numpy as np
import concourse.bass as bass
import concourse.mybir as mybir
from concourse.bass_utils import run_bass_kernel_spmd

F32 = mybir.dt.float32
BF16 = mybir.dt.bfloat16
U32 = mybir.dt.uint32
AF = mybir.ActivationFunctionType
ALU = mybir.AluOpType
AX = mybir.AxisListType
AP = bass.AP

NCORES = 8
T = 4096
NT = T // 128
D = 1024
H = 16
HD = 64
E = 8
FF = 4096
CAP = 1280
NSLOT = E * CAP
BIG = 2.0e9

DEBUG = False
BUILD_STAGE = 9


def build():
    nc = bass.Bass("TRN2", target_bir_lowering=False)

    x_in = nc.dram_tensor("x_in", [T, D], F32, kind="ExternalInput").ap()
    xT_in = nc.dram_tensor("xT_in", [D, T], F32, kind="ExternalInput").ap()
    wqt_in = nc.dram_tensor("wqt", [D, D], F32, kind="ExternalInput").ap()
    wkt_in = nc.dram_tensor("wkt", [D, D], F32, kind="ExternalInput").ap()
    wvt_in = nc.dram_tensor("wvt", [D, D], F32, kind="ExternalInput").ap()
    gwr_in = nc.dram_tensor("gwr", [128, 64], F32, kind="ExternalInput").ap()
    w1_in = nc.dram_tensor("w1", [D, FF], F32, kind="ExternalInput").ap()
    w2_in = nc.dram_tensor("w2", [FF, D], F32, kind="ExternalInput").ap()
    meta_in = nc.dram_tensor("meta", [128, 1], F32, kind="ExternalInput").ap()

    out_t = nc.dram_tensor("out", [T, D], F32, kind="ExternalOutput").ap()

    def scratch(name, shape, dtype):
        kind = "ExternalOutput" if DEBUG else "Internal"
        return nc.dram_tensor(name, shape, dtype, kind=kind).ap()

    eqp_d = scratch("eqp", [T, D], BF16)
    ek_d = scratch("ekd", [T, D], BF16)
    v_d = scratch("vd", [T, D], BF16)
    wt_d = scratch("wtd", [T, 256], BF16)
    att_d = scratch("attd", [T, D], BF16)
    y32_d = scratch("y32", [T, D], F32)
    ybf_d = scratch("ybf", [T, D], BF16)
    dbg_d = scratch("dbg", [128, 4096], F32)

    expin_d = scratch("expin", [NSLOT, D], BF16)
    rsout_d = scratch("rsout", [CAP, D], BF16)
    eout_d = scratch("eout", [CAP, D], BF16)
    agout_d = scratch("agout", [NSLOT + 128, D], BF16)
    cnt_in_d = nc.dram_tensor("cntin", [1, E], F32).ap()
    cnt_ag_d = nc.dram_tensor("cntag", [E, E], F32).ap()

    ctx = contextlib.ExitStack()

    def sbuf(name, shape, dtype):
        return ctx.enter_context(nc.sbuf_tensor(name, shape, dtype))

    def psum(name):
        return ctx.enter_context(nc.psum_tensor(name, [128, 512], F32))

    with ctx:
        dq_all = sbuf("dq_all", [128, NT * H], F32)
        pmax_all = sbuf("pmax_all", [128, NT], F32)
        slot_all = sbuf("slot_all", [128, NT], U32)
        lpos_all = sbuf("lpos_all", [128, NT], F32)
        oh_all = sbuf("oh_all", [128, NT * E], F32)
        iota8 = sbuf("iota8", [128, E], F32)
        iota64 = sbuf("iota64", [128, 64], F32)
        l2tri = sbuf("l2tri", [128, 128], F32)
        ones128 = sbuf("ones128", [128, 128], F32)
        maskB = sbuf("maskB", [8, 128], F32)
        b_sb = sbuf("b_sb", [128, E], F32)
        cnt_sb = sbuf("cnt_sb", [8, E], F32)
        meta_sb = sbuf("meta_sb", [128, 1], F32)
        gwc = sbuf("gwc", [128, 64], BF16)
        zer_bf = sbuf("zer_bf", [128, D], BF16)

        big_t = sbuf("big_t", [128, 1], F32)
        eps_t = sbuf("eps_t", [128, 1], F32)
        carry_sb = sbuf("carry_sb", [128, E], F32)
        et_all = sbuf("et_all", [128, NT], F32)

        ctx23 = contextlib.ExitStack()

        def sbuf23(name, shape, dtype):
            return ctx23.enter_context(nc.sbuf_tensor(name, shape, dtype))

        wqT = sbuf23("wqT", [128, 8 * D], BF16)
        wkT = sbuf23("wkT", [128, 8 * D], BF16)
        wvT = sbuf23("wvT", [128, 8 * D], BF16)

        xT_t = sbuf23("xT_t", [128, 2 * D], BF16)
        eq32 = sbuf23("eq32", [128, 2 * D], F32)
        ek32 = sbuf23("ek32", [128, 2 * D], F32)
        sd_t = sbuf23("sd_t", [128, 2 * HD], F32)
        eqp_t = sbuf23("eqp_t", [128, 2 * D], BF16)
        ekb_t = sbuf23("ekb_t", [128, 2 * D], BF16)
        vb_t = sbuf23("vb_t", [128, 2 * D], BF16)

        bd1 = sbuf23("bd1", [128, 2 * 2048], BF16)
        eq1 = sbuf23("eq1", [128, 2 * 1024], BF16)
        bd2 = sbuf23("bd2", [128, 2 * 2048], BF16)
        vs2 = sbuf23("vs2", [128, 2 * 1024], BF16)
        wsb = sbuf23("wsb", [128, 2 * 256], BF16)
        asb = sbuf23("asb", [128, 2 * 1024], BF16)


        PB = [psum(f"PB{i}") for i in range(8)]

        def sem(name):
            return ctx.enter_context(nc.semaphore(name))

        s_w0 = sem("s_w0")
        s_wq = sem("s_wq")
        s_ms = sem("s_ms")
        s_zin = sem("s_zin")
        s_xt = [sem("s_xt0"), sem("s_xt1")]
        s_pe2 = sem("s_pe2")
        s_ev2 = [sem(f"s_ev2_{i}") for i in range(4)]
        s_dve2 = sem("s_dve2")
        s_sto2 = [sem("s_sto20"), sem("s_sto21")]
        s_sto2g = [sem("s_sto2g0"), sem("s_sto2g1")]
        s_ld3 = [sem("s_ld30"), sem("s_ld31")]
        s_ld3b = [sem("s_ld3b0"), sem("s_ld3b1")]
        s_m1 = sem("s_m1")
        s_m1ev = sem("s_m1ev")
        s_m2 = sem("s_m2")
        s_m2ev = sem("s_m2ev")
        s_wtst = [sem("s_wtst0"), sem("s_wtst1")]
        s_attst = [sem("s_attst0"), sem("s_attst1")]
        s_ms3 = sem("s_ms3")
        s_ld4 = [sem("s_ld40"), sem("s_ld41")]
        s_yst = [sem("s_yst0"), sem("s_yst1")]
        s_yT = [sem("s_yT0"), sem("s_yT1")]
        s_va = sem("s_va")
        s_ssq = sem("s_ssq")
        s_ssq9 = sem("s_ssq9")
        s_rstd = sem("s_rstd")
        s_y = sem("s_y")
        s_gate = sem("s_gate")
        s_gev = sem("s_gev")
        s_lm = sem("s_lm")
        s_em = sem("s_em")
        s_ohg = sem("s_ohg")
        s_pa = sem("s_pa")
        s_5p = sem("s_5p")
        s_5r = sem("s_5r")
        s_cnt = sem("s_cnt")
        s_cc1 = sem("s_cc1")
        s_cc1b = sem("s_cc1b")
        s_bm = sem("s_bm")
        s_5b = sem("s_5b")
        s_ld6 = [sem("s_ld60"), sem("s_ld61")]
        s_sc6 = [sem("s_sc60"), sem("s_sc61")]
        s_cc2 = sem("s_cc2")
        s_w1l = sem("s_w1l")
        s_yTs = [sem("s_yTs0"), sem("s_yTs1")]
        s_mm1 = sem("s_mm1")
        s_h1 = sem("s_h1")
        s_mm2 = sem("s_mm2")
        s_eoev = sem("s_eoev")
        s_eost = [sem("s_eost0"), sem("s_eost1")]
        s_cc3 = sem("s_cc3")
        s_z9 = sem("s_z9")
        s_g9 = [sem("s_g90"), sem("s_g91")]
        s_ld9 = [sem("s_ld90"), sem("s_ld91")]
        s_va9 = sem("s_va9")
        s_rstd9 = sem("s_rstd9")
        s_o9 = sem("s_o9")
        s_ost9 = [sem("s_ost90"), sem("s_ost91")]
        s_dbg = sem("s_dbg")
        s_out = sem("s_out")
        s_chv = sem("s_chv")
        _chv = [0]
        s_cha = sem("s_cha")
        _cha = [0]

        def chain_v(v, inst):
            _chv[0] += 1
            inst.then_inc(s_chv, 1)
            v.wait_ge(s_chv, _chv[0])

        def chain_a(a, inst):
            _cha[0] += 1
            inst.then_inc(s_cha, 1)
            a.wait_ge(s_cha, _cha[0])

        with nc.Block("p0") as blk:

            @blk.gpsimd
            def _(g):
                g.iota(iota8[:, :], [[1, E]], base=0, channel_multiplier=0,
                       allow_small_or_imprecise_dtypes=True)
                g.iota(iota64[:, :], [[0, 8], [1, 8]], base=0,
                       channel_multiplier=0,
                       allow_small_or_imprecise_dtypes=True)
                g.iota(l2tri[:, :], [[1, 128]], base=0, channel_multiplier=-1,
                       allow_small_or_imprecise_dtypes=True)
                g.iota(maskB[0:8, :], [[0, 128]], base=0, channel_multiplier=1,
                       allow_small_or_imprecise_dtypes=True)
                g.memset(ones128[:, :], 1.0)
                g.memset(big_t[:, :], BIG)
                g.memset(eps_t[:, :], 1e-5)
                g.memset(carry_sb[:, :], 0.0)
                i = g.memset(zer_bf[:, :], 0.0)
                i.then_inc(s_ms, 1)
                i = g.dma_start(out=gwc[:, :], in_=gwr_in[:, :])
                i.then_inc(s_w0, 16)
                i = g.dma_start(out=meta_sb[:, :], in_=meta_in[:, :])
                i.then_inc(s_w0, 16)


        with nc.Block("p0b", no_gpsimd_drain=True) as blk:

            @blk.vector
            def _(v):
                v.tensor_single_scalar(l2tri[:, :], l2tri[:, :], 0.0, ALU.is_ge)
                v.wait_ge(s_w0, 32)
                v.tensor_scalar(maskB[0:8, :], maskB[0:8, :], meta_sb[0:8, 0:1],
                                None, ALU.is_lt)

        if BUILD_STAGE >= 2:
            DEPTH23 = 8
            _bu = [0, 0, 0]
            evac_cnt = []
            for _k in range(NT):
                row = []
                for _hi in range(6):
                    _b = (6 * _k + _hi) % 3
                    _bu[_b] += 1
                    row.append((_b, _bu[_b]))
                evac_cnt.append(row)

            with nc.Block("p23", no_gpsimd_drain=True) as blk:

                @blk.gpsimd
                def _(g):
                    for w_src, w_sb in ((wqt_in, wqT), (wkt_in, wkT),
                                        (wvt_in, wvT)):
                        for c in range(8):
                            i = g.dma_start(out=w_sb[:, D * c:D * (c + 1)],
                                            in_=w_src[128 * c:128 * (c + 1), :])
                            i.then_inc(s_wq, 16)
                    for k in range(NT + DEPTH23):
                        if k < NT:
                            if k >= 2:
                                g.wait_ge(s_pe2, 6 * (k - 1))
                            i = g.dma_start(
                                out=xT_t[:, D * (k % 2):D * (k % 2 + 1)].rearrange(
                                    "p (c t) -> p c t", c=8),
                                in_=AP(xT_in.tensor, 128 * k,
                                       [[T, 128], [128 * T, 8], [1, 128]]))
                            i.then_inc(s_xt[k % 2], 16)
                        if k >= 1 and k - 1 < NT:
                            j = k - 1
                            par = j % 2
                            o2 = D * par
                            g.wait_ge(s_dve2, j + 1)
                            for _hi in (4, 5):
                                _b, _c = evac_cnt[j][_hi]
                                g.wait_ge(s_ev2[_b], _c)
                            if j >= 2:
                                g.wait_ge(s_sto2g[par], 32 * (j // 2))
                            r0, r1 = 128 * j, 128 * (j + 1)
                            i = g.dma_start(out=ek_d[r0:r1, :],
                                            in_=ekb_t[:, o2:o2 + D])
                            i.then_inc(s_sto2g[par], 16)
                            i = g.dma_start(out=v_d[r0:r1, :],
                                            in_=vb_t[:, o2:o2 + D])
                            i.then_inc(s_sto2g[par], 16)
                        if k >= 4 and k - 4 < NT:
                            j = k - 4
                            par = j % 2
                            base = 128 * j
                            g.wait_ge(s_m1ev, j + 1)
                            if j >= 2:
                                g.wait_ge(s_wtst[par], 16 * (j // 2))
                            dst = AP(wt_d.tensor, base * 256,
                                     [[512, 4], [256, 2], [16, 16],
                                      [8 * 256, 16], [1, 16]])
                            i = g.dma_start(out=dst,
                                            in_=wsb[:, 256 * par:256 * (par + 1)]
                                            .rearrange("p (q h) -> p q h", h=16))
                            i.then_inc(s_wtst[par], 16)
                        if k >= 7 and k - 7 < NT:
                            j = k - 7
                            par = j % 2
                            g.wait_ge(s_m2ev, 2 * (j + 1))
                            if j >= 2:
                                g.wait_ge(s_attst[par], 32 * (j // 2))
                            for hb in range(2):
                                dst = AP(att_d.tensor, (128 * j + 64 * hb) * D,
                                         [[D, 8], [64, 16], [8 * D, 8], [1, 64]])
                                i = g.dma_start(
                                    out=dst,
                                    in_=asb[:, 1024 * par + 512 * hb:
                                            1024 * par + 512 * (hb + 1)]
                                    .rearrange("p (a e) -> p a e", e=64))
                                i.then_inc(s_attst[par], 16)

                @blk.sync
                def _(sy):
                    sy.wait_ge(s_ms3, 1)
                    for k in range(NT + DEPTH23):
                        if k >= 2 and k - 2 < NT:
                            j = k - 2
                            par = j % 2
                            base = 128 * j
                            sy.wait_ge(s_sto2[par], 16 * (j // 2 + 1))
                            sy.wait_ge(s_sto2g[par], 32 * (j // 2 + 1))
                            if j >= 2:
                                sy.wait_ge(s_m1, j - 1)
                            dst_e = bd1[0:64, 2048 * par:2048 * (par + 1)].rearrange(
                                "p (a c) -> p a c", c=32)[:, :, 0:16]
                            i = sy.dma_start(out=dst_e,
                                             in_=AP(ek_d.tensor, base * D,
                                                    [[16, 64], [2 * D, 64], [1, 16]]))
                            i.then_inc(s_ld3[par], 16)
                            dst_o = bd1[64:128, 2048 * par:2048 * (par + 1)].rearrange(
                                "p (a c) -> p a c", c=32)[:, :, 16:32]
                            i = sy.dma_start(out=dst_o,
                                             in_=AP(ek_d.tensor, (base + 1) * D,
                                                    [[16, 64], [2 * D, 64], [1, 16]]))
                            i.then_inc(s_ld3[par], 16)
                            i = sy.dma_start(
                                out=eq1[:, 1024 * par:1024 * (par + 1)].rearrange(
                                    "p (a h) -> p a h", h=16),
                                in_=AP(eqp_d.tensor, base * D,
                                       [[1024, 2], [16, 64], [2048, 64], [1, 16]]))
                            i.then_inc(s_ld3[par], 16)

                @blk.scalar
                def _(s):
                    s.wait_ge(s_ms3, 1)
                    for k in range(NT + DEPTH23):
                        if k >= 1 and k - 1 < NT:
                            j = k - 1
                            par = j % 2
                            o3 = D * par
                            s.wait_ge(s_dve2, j + 1)
                            if j >= 2:
                                s.wait_ge(s_sto2[par], 16 * (j // 2))
                            i = s.dma_start(out=eqp_d[128 * j:128 * (j + 1), :],
                                            in_=eqp_t[:, o3:o3 + D])
                            i.then_inc(s_sto2[par], 16)
                        if k < NT:
                            o = D * (k % 2)
                            if k >= 2:
                                s.wait_ge(s_dve2, k - 1)
                                s.wait_ge(s_sto2g[k % 2], 32 * (k // 2))
                            outs = (eq32[:, o:o + 512], eq32[:, o + 512:o + D],
                                    ek32[:, o:o + 512], ek32[:, o + 512:o + D],
                                    vb_t[:, o:o + 512], vb_t[:, o + 512:o + D])
                            for hi in range(6):
                                b, _c = evac_cnt[k][hi]
                                s.wait_ge(s_pe2, 6 * k + hi + 1)
                                if hi < 4:
                                    i = s.activation(outs[hi], PB[b][:, :], AF.Exp)
                                else:
                                    i = s.copy(outs[hi], PB[b][:, :])
                                i.then_inc(s_ev2[b], 1)
                        if k >= 5 and k - 5 < NT:
                            j = k - 5
                            par = j % 2
                            base = 128 * j
                            if j >= 2:
                                s.wait_ge(s_m2, j - 1)
                            s.wait_ge(s_wtst[par], 16 * (j // 2 + 1))
                            s.wait_ge(s_sto2g[par], 32 * (j // 2 + 1))
                            for j2 in range(8):
                                dst = bd2[16 * j2:16 * (j2 + 1),
                                          2048 * par:2048 * (par + 1)].rearrange(
                                    "p (a b) -> p a b", b=128)[:, :,
                                                              16 * j2:16 * (j2 + 1)]
                                i = s.dma_start(
                                    out=dst,
                                    in_=AP(wt_d.tensor, (base + j2) * 256,
                                           [[16, 16], [8 * 256, 16], [1, 16]]))
                                i.then_inc(s_ld3b[par], 16)
                            i = s.dma_start(
                                out=vs2[:, 1024 * par:1024 * (par + 1)].rearrange(
                                    "p (a e) -> p a e", e=64),
                                in_=AP(v_d.tensor, base * D,
                                       [[D, 8], [64, 16], [8 * D, 16], [1, 64]]))
                            i.then_inc(s_ld3b[par], 16)

                @blk.tensor
                def _(t):
                    t.wait_ge(s_wq, 24 * 16)
                    pe_bank_uses = [0, 0, 0]
                    for k in range(NT + DEPTH23):
                        if k < NT:
                            t.wait_ge(s_xt[k % 2], 16 * (k // 2 + 1))
                            for mi, wT in enumerate((wqT, wkT, wvT)):
                                for hf in range(2):
                                    hi = 2 * mi + hf
                                    b = (6 * k + hi) % 3
                                    if pe_bank_uses[b] > 0:
                                        t.wait_ge(s_ev2[b], pe_bank_uses[b])
                                    pe_bank_uses[b] += 1
                                    for c in range(8):
                                        lhsT = xT_t[:, D * (k % 2) + 128 * c:
                                                    D * (k % 2) + 128 * (c + 1)]
                                        i = t.matmul(PB[b][:, :], lhsT,
                                                     wT[:, D * c + 512 * hf:
                                                        D * c + 512 * (hf + 1)],
                                                     start=(c == 0), stop=(c == 7))
                                    i.then_inc(s_pe2, 1)
                        if k >= 3 and k - 3 < NT:
                            j = k - 3
                            par = j % 2
                            t.wait_ge(s_ld3[par], 48 * (j // 2 + 1))
                            if j >= 2:
                                t.wait_ge(s_m1ev, j - 1)
                            for jj in range(64):
                                c, q = jj % 4, jj // 4
                                i = t.matmul(
                                    PB[3 + par][32 * c:32 * c + 32,
                                                16 * q:16 * (q + 1)],
                                    bd1[:, 2048 * par + 32 * jj:
                                        2048 * par + 32 * (jj + 1)],
                                    eq1[:, 1024 * par + 16 * jj:
                                        1024 * par + 16 * (jj + 1)],
                                    start=True, stop=True,
                                    tile_position=(0, 32 * c))
                            i.then_inc(s_m1, 1)
                        if k >= 6 and k - 6 < NT:
                            j = k - 6
                            par = j % 2
                            t.wait_ge(s_ld3b[par], 144 * (j // 2 + 1))
                            t.wait_ge(s_m2ev, 2 * j)
                            for grp in range(16):
                                bank = PB[5 + grp // 8]
                                i = t.matmul(
                                    bank[:, 64 * (grp % 8):64 * (grp % 8 + 1)],
                                    bd2[:, 2048 * par + 128 * grp:
                                        2048 * par + 128 * (grp + 1)],
                                    vs2[:, 1024 * par + 64 * grp:
                                        1024 * par + 64 * (grp + 1)],
                                    start=True, stop=True)
                            i.then_inc(s_m2, 1)

                @blk.vector
                def _(v):
                    i = v.memset(bd1[:, :], 0.0)
                    i = v.memset(bd2[:, :], 0.0)
                    i.then_inc(s_ms3, 1)
                    for k in range(NT + DEPTH23):
                        if k < NT:
                            o = D * (k % 2)
                            oh = HD * (k % 2)
                            if k >= 2:
                                v.wait_ge(s_sto2[k % 2], 16 * (k // 2))
                                v.wait_ge(s_sto2g[k % 2], 32 * (k // 2))
                            for _hi in range(4):
                                _b, _c = evac_cnt[k][_hi]
                                v.wait_ge(s_ev2[_b], _c)
                            i = v.reduce_sum(sd_t[:, oh:oh + HD],
                                             ek32[:, o:o + D].rearrange(
                                                 "p (d h) -> p d h", h=H),
                                             axis=AX.X)
                            chain_v(v, i)
                            i = v.reciprocal(sd_t[:, oh:oh + HD],
                                             sd_t[:, oh:oh + HD])
                            chain_v(v, i)
                            v.tensor_tensor(
                                eqp_t[:, o:o + D].rearrange("p (d h) -> p d h",
                                                            h=H),
                                eq32[:, o:o + D].rearrange("p (d h) -> p d h",
                                                           h=H),
                                sd_t[:, oh:oh + HD].rearrange(
                                    "p (d u) -> p d u", u=1).broadcast_to(
                                    [128, HD, H]),
                                ALU.mult)
                            v.reduce_sum(dq_all[:, H * k:H * (k + 1)],
                                         eq32[:, o:o + D].rearrange(
                                             "p (d h) -> p h d", h=H),
                                         axis=AX.X)
                            i = v.tensor_copy(ekb_t[:, o:o + D],
                                              ek32[:, o:o + D])
                            i.then_inc(s_dve2, 1)
                        if k >= 3 and k - 3 < NT:
                            j = k - 3
                            par = j % 2
                            v.wait_ge(s_m1, j + 1)
                            if j >= 2:
                                v.wait_ge(s_wtst[par], 16 * (j // 2))
                            i = v.tensor_copy(wsb[:, 256 * par:256 * (par + 1)],
                                              PB[3 + par][:, 0:256])
                            i.then_inc(s_m1ev, 1)
                        if k >= 6 and k - 6 < NT:
                            j = k - 6
                            par = j % 2
                            v.wait_ge(s_m2, j + 1)
                            if j >= 2:
                                v.wait_ge(s_attst[par], 32 * (j // 2))
                            v.tensor_copy(asb[:, 1024 * par:1024 * par + 512],
                                          PB[5][:, :])
                            i = v.tensor_copy(asb[:, 1024 * par + 512:
                                                  1024 * par + 1024],
                                              PB[6][:, :])
                            i.then_inc(s_m2ev, 2)

        if BUILD_STAGE >= 4:
            ctx23.close()

            ctx45 = contextlib.ExitStack()

            def sbuf45(name, shape, dtype):
                return ctx45.enter_context(nc.sbuf_tensor(name, shape, dtype))

            xf_t = sbuf45("xf_t", [128, 2 * D], F32)
            ab_t = sbuf45("ab_t", [128, 2 * D], BF16)
            s1_t = sbuf45("s1_t", [128, 2 * D], F32)
            y32t = sbuf45("y32t", [128, 2 * D], F32)
            ybf_t = sbuf45("ybf_t", [128, 2 * D], BF16)
            yT_t = sbuf45("yT_t", [128, 2 * 8 * D], BF16)
            lgb = sbuf45("lgb", [128, 2 * 64], F32)
            lmb = sbuf45("lmb", [128, 2 * 64], F32)
            exb = sbuf45("exb", [128, 2 * 64], F32)
            c64 = sbuf45("c64", [128, 2 * 64], F32)
            em_t = sbuf45("em_t", [128, 2 * 8], F32)
            cum_t = sbuf45("cum_t", [128, 2 * 8], F32)
            ex8_t = sbuf45("ex8_t", [128, 2 * 8], F32)
            sc4 = sbuf45("sc4", [128, 32], F32)
            sq_t = sbuf45("sq_t", [128, D], F32)

            with nc.Block("p45", no_gpsimd_drain=True) as blk:

                @blk.sync
                def _(sy):
                    sy.wait_ge(s_attst[0], 32 * 16)
                    sy.wait_ge(s_attst[1], 32 * 16)
                    for k in range(NT + 2):
                        par = k % 2
                        if k < NT:
                            if k >= 2:
                                sy.wait_ge(s_y, k - 1)
                            i = sy.dma_start(out=xf_t[:, D * par:D * (par + 1)],
                                             in_=x_in[128 * k:128 * (k + 1), :])
                            i.then_inc(s_ld4[par], 16)
                            i = sy.dma_start(out=ab_t[:, D * par:D * (par + 1)],
                                             in_=att_d[128 * k:128 * (k + 1), :])
                            i.then_inc(s_ld4[par], 16)
                        if k >= 1 and k - 1 < NT:
                            kp = k - 1
                            pp = kp % 2
                            sy.wait_ge(s_y, kp + 1)
                            if kp >= 2:
                                sy.wait_ge(s_yst[pp], 32 * (kp // 2))
                            i = sy.dma_start(out=y32_d[128 * kp:128 * (kp + 1), :],
                                             in_=y32t[:, D * pp:D * (pp + 1)])
                            i.then_inc(s_yst[pp], 16)
                            i = sy.dma_start(out=ybf_d[128 * kp:128 * (kp + 1), :],
                                             in_=ybf_t[:, D * pp:D * (pp + 1)])
                            i.then_inc(s_yst[pp], 16)
                        if k >= 8 and (k - 8) % 8 == 0:
                            G = (k - 8) // 8
                            gp_ = G % 2
                            sy.wait_ge(s_yst[0], 32 * 4 * (G + 1))
                            sy.wait_ge(s_yst[1], 32 * 4 * (G + 1))
                            if G >= 2:
                                sy.wait_ge(s_gate, 8 * (G - 1))
                            for c in range(8):
                                i = sy.dma_start(
                                    out=yT_t[:, 8 * D * gp_ + 1024 * c:
                                             8 * D * gp_ + 1024 * (c + 1)],
                                    in_=ybf_d[1024 * G:1024 * (G + 1),
                                              128 * c:128 * (c + 1)],
                                    transpose=True)
                                i.then_inc(s_yT[gp_], 16)

                @blk.vector
                def _(v):
                    for k in range(NT + 10):
                        par = k % 2
                        o = D * par
                        so = 16 * par
                        if k < NT:
                            v.wait_ge(s_ld4[par], 32 * (k // 2 + 1))
                            if k >= 2:
                                v.wait_ge(s_yst[par], 32 * (k // 2))
                                v.wait_ge(s_rstd, k - 1)
                            i = v.reciprocal(sc4[:, so:so + 16],
                                             dq_all[:, H * k:H * (k + 1)])
                            chain_v(v, i)
                            i = v.tensor_tensor(
                                s1_t[:, o:o + D].rearrange("p (h e) -> p h e", e=HD),
                                ab_t[:, o:o + D].rearrange("p (h e) -> p h e", e=HD),
                                sc4[:, so:so + 16].rearrange(
                                    "p (h u) -> p h u", u=1).broadcast_to(
                                    [128, H, HD]),
                                ALU.mult)
                            chain_v(v, i)
                            i = v.tensor_add(s1_t[:, o:o + D], s1_t[:, o:o + D],
                                             xf_t[:, o:o + D])
                            i.then_inc(s_va, 1)
                            v.wait_ge(s_va, 3 * k + 1)
                            i = v.reduce_sum(sc4[:, so + 8:so + 9],
                                            s1_t[:, o:o + D], axis=AX.X)
                            chain_v(v, i)
                            i = v.tensor_scalar_mul(sc4[:, so + 9:so + 10],
                                                    sc4[:, so + 8:so + 9],
                                                    1.0 / D)
                            i.then_inc(s_va, 1)
                            v.wait_ge(s_va, 3 * k + 2)
                            i = v.tensor_mul(sc4[:, so + 13:so + 14],
                                             sc4[:, so + 9:so + 10],
                                             sc4[:, so + 9:so + 10])
                            chain_v(v, i)
                            v.wait_ge(s_ssq, k + 1)
                            i = v.scalar_tensor_tensor(sc4[:, so + 14:so + 15],
                                                       sc4[:, so + 15:so + 16],
                                                       1.0 / D,
                                                       sc4[:, so + 13:so + 14],
                                                       ALU.mult, ALU.subtract)
                            i.then_inc(s_va, 1)
                            v.wait_ge(s_rstd, k + 1)
                            i = v.scalar_tensor_tensor(
                                y32t[:, o:o + D], s1_t[:, o:o + D],
                                sc4[:, so + 9:so + 10],
                                sc4[:, so + 10:so + 11].broadcast_to([128, D]),
                                ALU.subtract, ALU.mult)
                            chain_v(v, i)
                            i = v.tensor_copy(ybf_t[:, o:o + D], y32t[:, o:o + D])
                            i.then_inc(s_y, 1)
                        if k >= 10 and k - 10 < NT:
                            kk = k - 10
                            pp = kk % 2
                            if kk % 8 == 0:
                                G = kk // 8
                                gp_ = G % 2
                                go = 64 * gp_
                                v.wait_ge(s_gate, 8 * (G + 1))
                                if G >= 2:
                                    v.wait_ge(s_em, G - 1)
                                    v.wait_ge(s_pa, G - 1)
                                i = v.tensor_copy(lgb[:, go:go + 64],
                                                  PB[6 + gp_][:, 0:64])
                                i.then_inc(s_gev, 1)
                                v.wait_ge(s_gev, G + 1)
                                i = v.reduce_max(em_t[:, 8 * gp_:8 * (gp_ + 1)],
                                                 lgb[:, go:go + 64].rearrange(
                                                     "p (g e) -> p g e", e=8),
                                                 axis=AX.X)
                                chain_v(v, i)
                                m8b = em_t[:, 8 * gp_:8 * (gp_ + 1)].rearrange(
                                    "p (g u) -> p g u", u=1).broadcast_to(
                                    [128, 8, 8])
                                i = v.tensor_tensor(
                                    lmb[:, go:go + 64].rearrange(
                                        "p (g e) -> p g e", e=8),
                                    lgb[:, go:go + 64].rearrange(
                                        "p (g e) -> p g e", e=8),
                                    m8b, ALU.subtract)
                                i.then_inc(s_lm, 1)
                                i = v.tensor_tensor(
                                    c64[:, go:go + 64].rearrange(
                                        "p (g e) -> p g e", e=8),
                                    lgb[:, go:go + 64].rearrange(
                                        "p (g e) -> p g e", e=8),
                                    m8b, ALU.is_equal)
                                chain_v(v, i)
                                i = v.tensor_scalar(c64[:, go:go + 64],
                                                    c64[:, go:go + 64],
                                                    -BIG, BIG,
                                                    ALU.mult, ALU.add)
                                chain_v(v, i)
                                i = v.tensor_add(c64[:, go:go + 64],
                                                 c64[:, go:go + 64],
                                                 iota64[:, :])
                                chain_v(v, i)
                                i = v.tensor_reduce(
                                    et_all[:, 8 * G:8 * (G + 1)],
                                    c64[:, go:go + 64].rearrange(
                                        "p (g e) -> p g e", e=8),
                                    AX.X, ALU.min)
                                chain_v(v, i)
                                etb = et_all[:, 8 * G:8 * (G + 1)].rearrange(
                                    "p (g u) -> p g u", u=1).broadcast_to(
                                    [128, 8, 8])
                                i = v.tensor_tensor(
                                    oh_all[:, 64 * G:64 * (G + 1)].rearrange(
                                        "p (g e) -> p g e", e=8),
                                    iota64[:, :].rearrange(
                                        "p (g e) -> p g e", e=8),
                                    etb, ALU.is_equal)
                                i.then_inc(s_ohg, 1)
                                v.wait_ge(s_em, G + 1)
                                i = v.reduce_sum(cum_t[:, 8 * gp_:8 * (gp_ + 1)],
                                                 exb[:, go:go + 64].rearrange(
                                                     "p (g e) -> p g e", e=8),
                                                 axis=AX.X)
                                chain_v(v, i)
                                i = v.reciprocal(pmax_all[:, 8 * G:8 * (G + 1)],
                                                 cum_t[:, 8 * gp_:8 * (gp_ + 1)])
                                i.then_inc(s_pa, 1)
                            v.wait_ge(s_5p, kk + 1)
                            i = v.tensor_mul(ex8_t[:, 8 * pp:8 * (pp + 1)],
                                             oh_all[:, 8 * kk:8 * (kk + 1)],
                                             PB[4 + pp][:, 0:8])
                            chain_v(v, i)
                            i = v.reduce_sum(lpos_all[:, kk:kk + 1],
                                             ex8_t[:, 8 * pp:8 * (pp + 1)],
                                             axis=AX.X)
                            chain_v(v, i)
                            i = v.tensor_scalar_add(lpos_all[:, kk:kk + 1],
                                                    lpos_all[:, kk:kk + 1], -1.0)
                            chain_v(v, i)
                            i = v.tensor_add(carry_sb[:, :], carry_sb[:, :],
                                             PB[4 + pp][:, 8:16])
                            i.then_inc(s_5r, 1)
                    v.wait_ge(s_5r, NT)
                    i = v.tensor_copy(cnt_sb[0:1, 0:8], carry_sb[0:1, 0:8])
                    i.then_inc(s_cnt, 1)

                @blk.gpsimd
                def _(g):
                    g.wait_ge(s_ms, 1)
                    for r in range(NSLOT // 128):
                        i = g.dma_start(out=expin_d[128 * r:128 * (r + 1), :],
                                        in_=zer_bf[:, :])
                        i.then_inc(s_zin, 16)
                    i = g.dma_start(out=agout_d[NSLOT:NSLOT + 128, :],
                                    in_=zer_bf[:, :])
                    i.then_inc(s_zin, 16)

                @blk.scalar
                def _(s):
                    for k in range(NT + 10):
                        par = k % 2
                        o = D * par
                        so = 16 * par
                        if k < NT:
                            s.wait_ge(s_va, 3 * k + 1)
                            i = s.activation(sq_t[:, :], s1_t[:, o:o + D],
                                             AF.Square,
                                             accum_out=sc4[:, so + 15:so + 16])
                            i.then_inc(s_ssq, 1)
                            s.wait_ge(s_va, 3 * k + 3)
                            i = s.activation(sc4[:, so + 14:so + 15],
                                             sc4[:, so + 14:so + 15], AF.Ln,
                                             bias=eps_t[:, :])
                            chain_a(s, i)
                            i = s.activation(sc4[:, so + 10:so + 11],
                                             sc4[:, so + 14:so + 15], AF.Exp,
                                             scale=-0.5)
                            i.then_inc(s_rstd, 1)
                        if k >= 10 and k - 10 < NT and (k - 10) % 8 == 0:
                            G = (k - 10) // 8
                            gp_ = G % 2
                            go = 64 * gp_
                            s.wait_ge(s_lm, G + 1)
                            if G >= 2:
                                s.wait_ge(s_pa, G - 1)
                            i = s.activation(exb[:, go:go + 64],
                                             lmb[:, go:go + 64], AF.Exp)
                            i.then_inc(s_em, 1)

                @blk.tensor
                def _(t):
                    for k in range(NT + 10):
                        if k >= 9 and (k - 9) % 8 == 0 and (k - 9) // 8 < NT // 8:
                            G = (k - 9) // 8
                            gp_ = G % 2
                            t.wait_ge(s_yT[gp_], 8 * 16 * (G // 2 + 1))
                            if G >= 2:
                                t.wait_ge(s_gev, G - 1)
                            for j in range(8):
                                kk = 8 * G + j
                                for c in range(8):
                                    i = t.matmul(
                                        PB[6 + gp_][:, 8 * j:8 * (j + 1)],
                                        yT_t[:, 8 * D * gp_ + 1024 * c + 128 * j:
                                             8 * D * gp_ + 1024 * c + 128 * (j + 1)],
                                        gwc[:, 8 * c:8 * (c + 1)],
                                        start=(c == 0), stop=(c == 7))
                                i.then_inc(s_gate, 1)
                            t.wait_ge(s_ohg, G + 1)
                            for j in range(8):
                                kk = 8 * G + j
                                pp = kk % 2
                                if kk >= 2:
                                    t.wait_ge(s_5r, kk - 1)
                                t.matmul(PB[4 + pp][:, 0:8], l2tri[:, :],
                                         oh_all[:, 8 * kk:8 * (kk + 1)],
                                         start=True, stop=False)
                                t.wait_ge(s_5r, kk)
                                t.matmul(PB[4 + pp][:, 0:8], ones128[0:1, :],
                                         carry_sb[0:1, 0:8],
                                         start=False, stop=True)
                                i = t.matmul(PB[4 + pp][:, 8:16], ones128[:, :],
                                             oh_all[:, 8 * kk:8 * (kk + 1)],
                                             start=True, stop=True)
                                i.then_inc(s_5p, 1)

        if BUILD_STAGE >= 4:
            ctx45.close()

            ctx67 = contextlib.ExitStack()

            def sbuf67(name, shape, dtype):
                return ctx67.enter_context(nc.sbuf_tensor(name, shape, dtype))

            em5 = sbuf67("em5", [128, 8], F32)
            sc5 = sbuf67("sc5", [128, 8], F32)
            ysc_t = sbuf67("ysc_t", [128, 2 * D], BF16)
            if BUILD_STAGE >= 9:
                w1big = sbuf67("w1big", [128, 8 * FF], BF16)
                w2big = sbuf67("w2big", [128, 32 * D], BF16)
                yTs = sbuf67("yTs", [128, 2 * 8 * 512], BF16)
                hT = sbuf67("hT", [128, 32 * 512], BF16)
                eo_t = sbuf67("eo_t", [128, 2 * D], BF16)

            with nc.Block("p5c", no_gpsimd_drain=True) as blk:

                @blk.gpsimd
                def _(g):
                    i = g.dma_start(out=cnt_in_d[:, :], in_=cnt_sb[0:1, 0:8])
                    i.then_inc(s_cc1, 16)
                    g.wait_ge(s_cc1, 16)
                    i = g.collective_compute(
                        "AllGather", ALU.bypass,
                        replica_groups=[list(range(NCORES))],
                        ins=[cnt_in_d], outs=[cnt_ag_d])
                    i.then_inc(s_cc1b, 1)
                    g.wait_ge(s_cc1b, 1)
                    i = g.dma_start(out=cnt_sb[0:8, 0:8], in_=cnt_ag_d[:, :])
                    i.then_inc(s_cc1, 16)

                @blk.tensor
                def _(t):
                    t.wait_ge(s_cc1, 32)
                    i = t.matmul(PB[4][:, 0:8], maskB[0:8, :], cnt_sb[0:8, 0:8],
                                 start=True, stop=True)
                    i.then_inc(s_bm, 1)

                @blk.vector
                def _(v):
                    v.wait_ge(s_bm, 1)
                    v.tensor_copy(b_sb[:, :], PB[4][:, 0:8])

            with nc.Block("p5d", no_gpsimd_drain=True) as blk:

                @blk.vector
                def _(v):
                    for kk in range(NT):
                        i = v.tensor_mul(em5[:, 0:8], oh_all[:, 8 * kk:8 * (kk + 1)],
                                         b_sb[:, :])
                        chain_v(v, i)
                        i = v.reduce_sum(sc5[:, 0:1], em5[:, 0:8], axis=AX.X)
                        chain_v(v, i)
                        i = v.tensor_add(sc5[:, 0:1], sc5[:, 0:1],
                                         lpos_all[:, kk:kk + 1])
                        chain_v(v, i)
                        i = v.tensor_single_scalar(sc5[:, 1:2], sc5[:, 0:1],
                                                   float(CAP), ALU.is_ge)
                        chain_v(v, i)
                        i = v.scalar_tensor_tensor(sc5[:, 2:3],
                                                   et_all[:, kk:kk + 1],
                                                   float(CAP),
                                                   sc5[:, 0:1],
                                                   ALU.mult, ALU.add)
                        chain_v(v, i)
                        i = v.scalar_tensor_tensor(sc5[:, 3:4],
                                                   sc5[:, 1:2], BIG,
                                                   sc5[:, 2:3],
                                                   ALU.mult, ALU.add)
                        chain_v(v, i)
                        i = v.tensor_single_scalar(sc5[:, 3:4], sc5[:, 3:4],
                                                   float(NSLOT), ALU.min)
                        chain_v(v, i)
                        i = v.tensor_copy(slot_all[:, kk:kk + 1], sc5[:, 3:4])
                        i.then_inc(s_5b, 1)

                @blk.sync
                def _(sy):
                    for kk in range(NT):
                        par = kk % 2
                        if kk >= 2:
                            sy.wait_ge(s_sc6[par], 16 * (kk // 2))
                        i = sy.dma_start(out=ysc_t[:, D * par:D * (par + 1)],
                                         in_=ybf_d[128 * kk:128 * (kk + 1), :])
                        i.then_inc(s_ld6[par], 16)

                @blk.gpsimd
                def _(g):
                    g.wait_ge(s_zin, 16 * (NSLOT // 128 + 1))
                    for kk in range(NT):
                        par = kk % 2
                        g.wait_ge(s_5b, kk + 1)
                        g.wait_ge(s_ld6[par], 16 * (kk // 2 + 1))
                        i = g.indirect_dma_start(
                            out=expin_d[:, :],
                            out_offset=bass.IndirectOffsetOnAxis(
                                ap=slot_all[:, kk:kk + 1], axis=0),
                            in_=ysc_t[:, D * par:D * (par + 1)],
                            in_offset=None,
                            bounds_check=NSLOT - 1, oob_is_err=False)
                        i.then_inc(s_sc6[par], 16)
                    if BUILD_STAGE >= 9:
                        for c in range(8):
                            i = g.dma_start(out=w1big[:, FF * c:FF * (c + 1)],
                                            in_=w1_in[128 * c:128 * (c + 1), :])
                            i.then_inc(s_w1l, 16)
                        for f in range(32):
                            i = g.dma_start(out=w2big[:, D * f:D * (f + 1)],
                                            in_=w2_in[128 * f:128 * (f + 1), :])
                            i.then_inc(s_w1l, 16)
                    g.wait_ge(s_sc6[0], 16 * (NT // 2))
                    g.wait_ge(s_sc6[1], 16 * (NT // 2))
                    i = g.collective_compute(
                        "ReduceScatter", ALU.add,
                        replica_groups=[list(range(NCORES))],
                        ins=[expin_d], outs=[rsout_d])
                    i.then_inc(s_cc2, 1)

        if BUILD_STAGE >= 9:
            with nc.Block("p7", no_gpsimd_drain=True) as blk:
                MSZ = [512, 512, 256]

                @blk.sync
                def _(sy):
                    sy.wait_ge(s_cc2, 1)
                    for m in range(3):
                        msz = MSZ[m]
                        par = m % 2
                        if m >= 2:
                            sy.wait_ge(s_mm1, 32 * (m - 1))
                        for c in range(8):
                            i = sy.dma_start(
                                out=yTs[:, 4096 * par + 512 * c:
                                        4096 * par + 512 * c + msz],
                                in_=rsout_d[512 * m:512 * m + msz,
                                            128 * c:128 * (c + 1)],
                                transpose=True)
                            i.then_inc(s_yTs[par], 16)

                @blk.tensor
                def _(t):
                    t.wait_ge(s_w1l, 40 * 16)
                    stg = 0
                    for m in range(3):
                        msz = MSZ[m]
                        par = m % 2
                        t.wait_ge(s_yTs[par], 8 * 16 * (m // 2 + 1))
                        for f in range(32):
                            fi = 32 * m + f
                            if fi >= 2:
                                t.wait_ge(s_h1, fi - 1)
                            for d_ in range(8):
                                i = t.matmul(
                                    PB[f % 2][:, 0:msz],
                                    w1big[:, FF * d_ + 128 * f:
                                          FF * d_ + 128 * (f + 1)],
                                    yTs[:, 4096 * par + 512 * d_:
                                        4096 * par + 512 * d_ + msz],
                                    start=(d_ == 0), stop=(d_ == 7))
                            i.then_inc(s_mm1, 1)
                        t.wait_ge(s_h1, 32 * (m + 1))
                        for st in range(msz // 128):
                            pp2 = stg % 2
                            bk = (PB[2 + 2 * pp2], PB[3 + 2 * pp2])
                            if stg >= 2:
                                t.wait_ge(s_eoev, 2 * (stg - 1))
                            for f in range(32):
                                for hf in range(2):
                                    i = t.matmul(
                                        bk[hf][:, :],
                                        hT[:, 512 * f + 128 * st:
                                           512 * f + 128 * (st + 1)],
                                        w2big[:, D * f + 512 * hf:
                                              D * f + 512 * (hf + 1)],
                                        start=(f == 0), stop=(f == 31))
                            i.then_inc(s_mm2, 1)
                            stg += 1

                @blk.scalar
                def _(s):
                    for m in range(3):
                        msz = MSZ[m]
                        for f in range(32):
                            fi = 32 * m + f
                            s.wait_ge(s_mm1, fi + 1)
                            i = s.activation(hT[:, 512 * f:512 * f + msz],
                                             PB[f % 2][:, 0:msz], AF.Relu)
                            i.then_inc(s_h1, 1)

                @blk.vector
                def _(v):
                    for stg in range(10):
                        pp2 = stg % 2
                        v.wait_ge(s_mm2, stg + 1)
                        if stg >= 2:
                            v.wait_ge(s_eost[pp2], 16 * (stg // 2))
                        v.tensor_copy(eo_t[:, D * pp2:D * pp2 + 512],
                                      PB[2 + 2 * pp2][:, :])
                        i = v.tensor_copy(eo_t[:, D * pp2 + 512:D * (pp2 + 1)],
                                          PB[3 + 2 * pp2][:, :])
                        i.then_inc(s_eoev, 2)

                @blk.gpsimd
                def _(g):
                    for stg in range(10):
                        pp2 = stg % 2
                        g.wait_ge(s_eoev, 2 * (stg + 1))
                        if stg >= 2:
                            g.wait_ge(s_eost[pp2], 16 * (stg // 2))
                        i = g.dma_start(out=eout_d[128 * stg:128 * (stg + 1), :],
                                        in_=eo_t[:, D * pp2:D * (pp2 + 1)])
                        i.then_inc(s_eost[pp2], 16)
                    g.wait_ge(s_eost[0], 16 * 5)
                    g.wait_ge(s_eost[1], 16 * 5)
                    i = g.collective_compute(
                        "AllGather", ALU.bypass,
                        replica_groups=[list(range(NCORES))],
                        ins=[eout_d], outs=[agout_d[0:NSLOT, :]])
                    i.then_inc(s_cc3, 1)

            ctx67.close()

            ctx9 = contextlib.ExitStack()

            def sbuf9(name, shape, dtype):
                return ctx9.enter_context(nc.sbuf_tensor(name, shape, dtype))

            gat_t = sbuf9("gat_t", [128, 2 * D], BF16)
            yf9_t = sbuf9("yf9_t", [128, 2 * D], F32)
            o9_t = sbuf9("o9_t", [128, 2 * D], F32)
            sq9 = sbuf9("sq9", [128, D], F32)
            sc9 = sbuf9("sc9", [128, 32], F32)

            with nc.Block("p9", no_gpsimd_drain=True) as blk:

                @blk.gpsimd
                def _(g):
                    g.wait_ge(s_cc3, 1)
                    for kk in range(NT):
                        par = kk % 2
                        if kk >= 2:
                            g.wait_ge(s_o9, kk - 1)
                        i = g.indirect_dma_start(
                            out=gat_t[:, D * par:D * (par + 1)],
                            out_offset=None,
                            in_=agout_d[:, :],
                            in_offset=bass.IndirectOffsetOnAxis(
                                ap=slot_all[:, kk:kk + 1], axis=0))
                        i.then_inc(s_g9[par], 16)

                @blk.sync
                def _(sy):
                    for kk in range(NT + 1):
                        par = kk % 2
                        if kk < NT:
                            if kk >= 2:
                                sy.wait_ge(s_o9, kk - 1)
                            i = sy.dma_start(
                                out=yf9_t[:, D * par:D * (par + 1)],
                                in_=y32_d[128 * kk:128 * (kk + 1), :])
                            i.then_inc(s_ld9[par], 16)
                        if kk >= 1:
                            kp = kk - 1
                            pp = kp % 2
                            sy.wait_ge(s_o9, kp + 1)
                            if kp >= 2:
                                sy.wait_ge(s_ost9[pp], 16 * (kp // 2))
                            i = sy.dma_start(
                                out=out_t[128 * kp:128 * (kp + 1), :],
                                in_=o9_t[:, D * pp:D * (pp + 1)])
                            i.then_inc(s_ost9[pp], 16)

                @blk.vector
                def _(v):
                    for kk in range(NT):
                        par = kk % 2
                        so = 16 * par
                        v.wait_ge(s_g9[par], 16 * (kk // 2 + 1))
                        v.wait_ge(s_ld9[par], 16 * (kk // 2 + 1))
                        if kk >= 2:
                            v.wait_ge(s_ost9[par], 16 * (kk // 2))
                            v.wait_ge(s_rstd9, kk - 1)
                        i = v.tensor_scalar(o9_t[:, D * par:D * (par + 1)],
                                            gat_t[:, D * par:D * (par + 1)],
                                            pmax_all[:, kk:kk + 1], None,
                                            ALU.mult)
                        chain_v(v, i)
                        i = v.tensor_add(o9_t[:, D * par:D * (par + 1)],
                                         o9_t[:, D * par:D * (par + 1)],
                                         yf9_t[:, D * par:D * (par + 1)])
                        i.then_inc(s_va9, 1)
                        v.wait_ge(s_va9, 3 * kk + 1)
                        i = v.reduce_sum(sc9[:, so:so + 1],
                                         o9_t[:, D * par:D * (par + 1)],
                                         axis=AX.X)
                        chain_v(v, i)
                        i = v.tensor_scalar_mul(sc9[:, so + 1:so + 2],
                                                sc9[:, so:so + 1], 1.0 / D)
                        i.then_inc(s_va9, 1)
                        v.wait_ge(s_va9, 3 * kk + 2)
                        i = v.tensor_mul(sc9[:, so + 5:so + 6],
                                         sc9[:, so + 1:so + 2],
                                         sc9[:, so + 1:so + 2])
                        chain_v(v, i)
                        v.wait_ge(s_ssq9, kk + 1)
                        i = v.scalar_tensor_tensor(sc9[:, so + 2:so + 3],
                                                   sc9[:, so + 6:so + 7],
                                                   1.0 / D,
                                                   sc9[:, so + 5:so + 6],
                                                   ALU.mult, ALU.subtract)
                        i.then_inc(s_va9, 1)
                        v.wait_ge(s_rstd9, kk + 1)
                        i = v.scalar_tensor_tensor(
                            o9_t[:, D * par:D * (par + 1)],
                            o9_t[:, D * par:D * (par + 1)],
                            sc9[:, so + 1:so + 2],
                            sc9[:, so + 4:so + 5].broadcast_to([128, D]),
                            ALU.subtract, ALU.mult)
                        i.then_inc(s_o9, 1)

                @blk.scalar
                def _(s):
                    for kk in range(NT):
                        par = kk % 2
                        so = 16 * par
                        s.wait_ge(s_va9, 3 * kk + 1)
                        i = s.activation(sq9[:, :], o9_t[:, D * par:D * (par + 1)],
                                         AF.Square,
                                         accum_out=sc9[:, so + 6:so + 7])
                        i.then_inc(s_ssq9, 1)
                        s.wait_ge(s_va9, 3 * kk + 3)
                        i = s.activation(sc9[:, so + 2:so + 3],
                                         sc9[:, so + 2:so + 3], AF.Ln,
                                         bias=eps_t[:, :])
                        chain_a(s, i)
                        i = s.activation(sc9[:, so + 4:so + 5],
                                         sc9[:, so + 2:so + 3], AF.Exp,
                                         scale=-0.5)
                        i.then_inc(s_rstd9, 1)

            ctx9.close()

        if BUILD_STAGE < 4:
            ctx23.close()
        with nc.Block("pend", no_gpsimd_drain=True) as blk:

            @blk.gpsimd
            def _(g):
                g.wait_ge(s_sto2[0], 16 * (NT // 2))
                g.wait_ge(s_sto2[1], 16 * (NT // 2))
                g.wait_ge(s_sto2g[0], 32 * (NT // 2))
                g.wait_ge(s_sto2g[1], 32 * (NT // 2))
                if BUILD_STAGE >= 3:
                    g.wait_ge(s_attst[0], 32 * 16)
                    g.wait_ge(s_attst[1], 32 * 16)
                if BUILD_STAGE >= 4:
                    g.wait_ge(s_yst[0], 32 * 16)
                    g.wait_ge(s_yst[1], 32 * 16)
                if BUILD_STAGE >= 9:
                    g.wait_ge(s_ost9[0], 16 * (NT // 2))
                    g.wait_ge(s_ost9[1], 16 * (NT // 2))
                    if DEBUG:
                        i = g.dma_start(out=dbg_d[:, 512:544], in_=pmax_all[:, :])
                        i.then_inc(s_dbg, 16)
                        i = g.dma_start(out=dbg_d[:, 544:576], in_=lpos_all[:, :])
                        i.then_inc(s_dbg, 16)
                        i = g.dma_start(out=dbg_d[:, 576:608], in_=et_all[:, :])
                        i.then_inc(s_dbg, 16)
                        i = g.dma_start(out=dbg_d[:, 608:640], in_=slot_all[:, :])
                        i.then_inc(s_dbg, 16)
                        i = g.dma_start(out=dbg_d[:, 640:648], in_=b_sb[:, :])
                        i.then_inc(s_dbg, 16)
                        i = g.dma_start(out=dbg_d[0:8, 648:656],
                                        in_=cnt_sb[0:8, 0:8])
                        i.then_inc(s_dbg, 16)
                if DEBUG:
                    i = g.dma_start(out=dbg_d[:, 0:NT * H], in_=dq_all[:, :])
                    i.then_inc(s_dbg, 16)
                if BUILD_STAGE < 9:
                    for k in range(NT):
                        i = g.dma_start(
                            out=out_t[128 * k:128 * (k + 1), 0:D],
                            in_=zer_bf[:, :])
                        i.then_inc(s_out, 16)
                g.wait_ge(s_out, 16 * NT if BUILD_STAGE < 9 else 0)
                if DEBUG:
                    g.wait_ge(s_dbg, 16 * (7 if BUILD_STAGE >= 4 else 1))

    return nc


def _stage_inputs(L0_attn_input_seq_embs, L0_attn_W_Q, L0_attn_W_K,
                  L0_attn_W_V, L0_ffn_gate_w, L0_ffn_w1, L0_ffn_w2):
    x = np.ascontiguousarray(L0_attn_input_seq_embs.reshape(32768, 1024))
    xT = np.ascontiguousarray(x.T)
    op = np.arange(1024)
    perm = (op % 16) * 64 + op // 16
    wqt = np.ascontiguousarray(L0_attn_W_Q.T[:, perm])
    wkt = np.ascontiguousarray(L0_attn_W_K.T[:, perm])
    wvt = np.ascontiguousarray(L0_attn_W_V.T)
    gwr = np.ascontiguousarray(
        L0_ffn_gate_w.reshape(8, 128, 8).transpose(1, 0, 2).reshape(128, 64))
    in_maps = []
    for i in range(NCORES):
        in_maps.append({
            "x_in": x[T * i:T * (i + 1)],
            "xT_in": np.ascontiguousarray(xT[:, T * i:T * (i + 1)]),
            "wqt": wqt, "wkt": wkt, "wvt": wvt, "gwr": gwr,
            "w1": np.ascontiguousarray(L0_ffn_w1[i]),
            "w2": np.ascontiguousarray(L0_ffn_w2[i]),
            "meta": np.full((128, 1), float(i), dtype=np.float32),
        })
    return in_maps


def kernel(L0_attn_input_seq_embs, L0_attn_W_Q, L0_attn_W_K, L0_attn_W_V,
           L0_ffn_gate_w, L0_ffn_w1, L0_ffn_w2):
    in_maps = _stage_inputs(L0_attn_input_seq_embs, L0_attn_W_Q, L0_attn_W_K,
                            L0_attn_W_V, L0_ffn_gate_w, L0_ffn_w1, L0_ffn_w2)
    nc = build()
    res = run_bass_kernel_spmd(nc, in_maps, core_ids=list(range(NCORES)))
    out = np.concatenate([r["out"] for r in res.results], axis=0)
    return out.reshape(2, 16384, 1024)


# version 11
# speedup vs baseline: 1.0158x; 1.0069x over previous
import contextlib
import numpy as np
import concourse.bass as bass
import concourse.mybir as mybir
from concourse.bass_utils import run_bass_kernel_spmd

F32 = mybir.dt.float32
BF16 = mybir.dt.bfloat16
U32 = mybir.dt.uint32
AF = mybir.ActivationFunctionType
ALU = mybir.AluOpType
AX = mybir.AxisListType
AP = bass.AP

NCORES = 8
T = 4096
NT = T // 128
D = 1024
H = 16
HD = 64
E = 8
FF = 4096
CAP = 1280
NSLOT = E * CAP
BIG = 2.0e9

DEBUG = False
BUILD_STAGE = 9


def build():
    nc = bass.Bass("TRN2", target_bir_lowering=False)

    x_in = nc.dram_tensor("x_in", [T, D], F32, kind="ExternalInput").ap()
    xT_in = nc.dram_tensor("xT_in", [D, T], F32, kind="ExternalInput").ap()
    wqt_in = nc.dram_tensor("wqt", [D, D], F32, kind="ExternalInput").ap()
    wkt_in = nc.dram_tensor("wkt", [D, D], F32, kind="ExternalInput").ap()
    wvt_in = nc.dram_tensor("wvt", [D, D], F32, kind="ExternalInput").ap()
    gwr_in = nc.dram_tensor("gwr", [128, 64], F32, kind="ExternalInput").ap()
    w1_in = nc.dram_tensor("w1", [D, FF], F32, kind="ExternalInput").ap()
    w2_in = nc.dram_tensor("w2", [FF, D], F32, kind="ExternalInput").ap()
    meta_in = nc.dram_tensor("meta", [128, 1], F32, kind="ExternalInput").ap()

    out_t = nc.dram_tensor("out", [T, D], F32, kind="ExternalOutput").ap()

    def scratch(name, shape, dtype):
        kind = "ExternalOutput" if DEBUG else "Internal"
        return nc.dram_tensor(name, shape, dtype, kind=kind).ap()

    eqp_d = scratch("eqp", [T, D], BF16)
    ek_d = scratch("ekd", [T, D], BF16)
    v_d = scratch("vd", [T, D], BF16)
    wt_d = scratch("wtd", [T, 256], BF16)
    att_d = scratch("attd", [T, D], BF16)
    y32_d = scratch("y32", [T, D], F32)
    ybf_d = scratch("ybf", [T, D], BF16)
    dbg_d = scratch("dbg", [128, 4096], F32)

    expin_d = scratch("expin", [NSLOT, D], BF16)
    rsout_d = scratch("rsout", [CAP, D], BF16)
    eout_d = scratch("eout", [CAP, D], BF16)
    agout_d = scratch("agout", [NSLOT + 128, D], BF16)
    cnt_in_d = nc.dram_tensor("cntin", [1, E], F32).ap()
    cnt_ag_d = nc.dram_tensor("cntag", [E, E], F32).ap()

    ctx = contextlib.ExitStack()

    def sbuf(name, shape, dtype):
        return ctx.enter_context(nc.sbuf_tensor(name, shape, dtype))

    def psum(name):
        return ctx.enter_context(nc.psum_tensor(name, [128, 512], F32))

    with ctx:
        dq_all = sbuf("dq_all", [128, NT * H], F32)
        pmax_all = sbuf("pmax_all", [128, NT], F32)
        slot_all = sbuf("slot_all", [128, NT], U32)
        lpos_all = sbuf("lpos_all", [128, NT], F32)
        oh_all = sbuf("oh_all", [128, NT * E], F32)
        iota8 = sbuf("iota8", [128, E], F32)
        iota64 = sbuf("iota64", [128, 64], F32)
        l2tri = sbuf("l2tri", [128, 128], F32)
        ones128 = sbuf("ones128", [128, 128], F32)
        maskB = sbuf("maskB", [8, 128], F32)
        b_sb = sbuf("b_sb", [128, E], F32)
        cnt_sb = sbuf("cnt_sb", [8, E], F32)
        meta_sb = sbuf("meta_sb", [128, 1], F32)
        gwc = sbuf("gwc", [128, 64], BF16)
        zer_bf = sbuf("zer_bf", [128, D], BF16)

        big_t = sbuf("big_t", [128, 1], F32)
        eps_t = sbuf("eps_t", [128, 1], F32)
        carry_sb = sbuf("carry_sb", [128, E], F32)
        et_all = sbuf("et_all", [128, NT], F32)

        ctx23 = contextlib.ExitStack()

        def sbuf23(name, shape, dtype):
            return ctx23.enter_context(nc.sbuf_tensor(name, shape, dtype))

        wqT = sbuf23("wqT", [128, 8 * D], BF16)
        wkT = sbuf23("wkT", [128, 8 * D], BF16)
        wvT = sbuf23("wvT", [128, 8 * D], BF16)

        xT_t = sbuf23("xT_t", [128, 2 * D], BF16)
        eq32 = sbuf23("eq32", [128, 2 * D], F32)
        ek32 = sbuf23("ek32", [128, 2 * D], F32)
        sd_t = sbuf23("sd_t", [128, 2 * HD], F32)
        eqp_t = sbuf23("eqp_t", [128, 2 * D], BF16)
        ekb_t = sbuf23("ekb_t", [128, 2 * D], BF16)
        vb_t = sbuf23("vb_t", [128, 2 * D], BF16)

        bd1 = sbuf23("bd1", [128, 2 * 2048], BF16)
        eq1 = sbuf23("eq1", [128, 2 * 1024], BF16)
        bd2 = sbuf23("bd2", [128, 2 * 2048], BF16)
        vs2 = sbuf23("vs2", [128, 2 * 1024], BF16)
        wsb = sbuf23("wsb", [128, 2 * 256], BF16)
        asb = sbuf23("asb", [128, 2 * 1024], BF16)


        PB = [psum(f"PB{i}") for i in range(8)]

        def sem(name):
            return ctx.enter_context(nc.semaphore(name))

        s_w0 = sem("s_w0")
        s_wq = sem("s_wq")
        s_ms = sem("s_ms")
        s_zin = sem("s_zin")
        s_xt = [sem("s_xt0"), sem("s_xt1")]
        s_pe2 = sem("s_pe2")
        s_ev2 = [sem(f"s_ev2_{i}") for i in range(4)]
        s_dve2 = sem("s_dve2")
        s_sto2 = [sem("s_sto20"), sem("s_sto21")]
        s_sto2g = [sem("s_sto2g0"), sem("s_sto2g1")]
        s_ld3 = [sem("s_ld30"), sem("s_ld31")]
        s_ld3b = [sem("s_ld3b0"), sem("s_ld3b1")]
        s_m1 = sem("s_m1")
        s_m1ev = sem("s_m1ev")
        s_m2 = sem("s_m2")
        s_m2ev = sem("s_m2ev")
        s_wtst = [sem("s_wtst0"), sem("s_wtst1")]
        s_attst = [sem("s_attst0"), sem("s_attst1")]
        s_ms3 = sem("s_ms3")
        s_ld4 = [sem("s_ld40"), sem("s_ld41")]
        s_yst = [sem("s_yst0"), sem("s_yst1")]
        s_yT = [sem("s_yT0"), sem("s_yT1")]
        s_va = sem("s_va")
        s_ssq = sem("s_ssq")
        s_ssq9 = sem("s_ssq9")
        s_rstd = sem("s_rstd")
        s_y = sem("s_y")
        s_ap = sem("s_ap")
        s_gate = sem("s_gate")
        s_gev = sem("s_gev")
        s_lm = sem("s_lm")
        s_em = sem("s_em")
        s_ohg = sem("s_ohg")
        s_pa = sem("s_pa")
        s_5p = sem("s_5p")
        s_5r = sem("s_5r")
        s_cnt = sem("s_cnt")
        s_cc1 = sem("s_cc1")
        s_cc1b = sem("s_cc1b")
        s_bm = sem("s_bm")
        s_5b = sem("s_5b")
        s_ld6 = [sem("s_ld60"), sem("s_ld61")]
        s_sc6 = [sem("s_sc60"), sem("s_sc61")]
        s_cc2 = sem("s_cc2")
        s_w1l = sem("s_w1l")
        s_yTs = [sem("s_yTs0"), sem("s_yTs1")]
        s_mm1 = sem("s_mm1")
        s_h1 = sem("s_h1")
        s_mm2 = sem("s_mm2")
        s_eoev = sem("s_eoev")
        s_eost = [sem("s_eost0"), sem("s_eost1")]
        s_cc3 = sem("s_cc3")
        s_z9 = sem("s_z9")
        s_g9 = [sem("s_g90"), sem("s_g91")]
        s_ld9 = [sem("s_ld90"), sem("s_ld91")]
        s_va9 = sem("s_va9")
        s_rstd9 = sem("s_rstd9")
        s_o9 = sem("s_o9")
        s_ost9 = [sem("s_ost90"), sem("s_ost91")]
        s_dbg = sem("s_dbg")
        s_out = sem("s_out")
        s_chv = sem("s_chv")
        _chv = [0]
        s_cha = sem("s_cha")
        _cha = [0]

        def chain_v(v, inst):
            _chv[0] += 1
            inst.then_inc(s_chv, 1)
            v.wait_ge(s_chv, _chv[0])

        def chain_a(a, inst):
            _cha[0] += 1
            inst.then_inc(s_cha, 1)
            a.wait_ge(s_cha, _cha[0])

        with nc.Block("p0") as blk:

            @blk.gpsimd
            def _(g):
                g.iota(iota8[:, :], [[1, E]], base=0, channel_multiplier=0,
                       allow_small_or_imprecise_dtypes=True)
                g.iota(iota64[:, :], [[0, 8], [1, 8]], base=0,
                       channel_multiplier=0,
                       allow_small_or_imprecise_dtypes=True)
                g.iota(l2tri[:, :], [[1, 128]], base=0, channel_multiplier=-1,
                       allow_small_or_imprecise_dtypes=True)
                g.iota(maskB[0:8, :], [[0, 128]], base=0, channel_multiplier=1,
                       allow_small_or_imprecise_dtypes=True)
                g.memset(ones128[:, :], 1.0)
                g.memset(big_t[:, :], BIG)
                g.memset(eps_t[:, :], 1e-5)
                g.memset(carry_sb[:, :], 0.0)
                i = g.memset(zer_bf[:, :], 0.0)
                i.then_inc(s_ms, 1)
                i = g.dma_start(out=gwc[:, :], in_=gwr_in[:, :])
                i.then_inc(s_w0, 16)
                i = g.dma_start(out=meta_sb[:, :], in_=meta_in[:, :])
                i.then_inc(s_w0, 16)


        with nc.Block("p0b", no_gpsimd_drain=True) as blk:

            @blk.vector
            def _(v):
                v.tensor_single_scalar(l2tri[:, :], l2tri[:, :], 0.0, ALU.is_ge)
                v.wait_ge(s_w0, 32)
                v.tensor_scalar(maskB[0:8, :], maskB[0:8, :], meta_sb[0:8, 0:1],
                                None, ALU.is_lt)

        if BUILD_STAGE >= 2:
            DEPTH23 = 8
            _bu = [0, 0, 0]
            evac_cnt = []
            for _k in range(NT):
                row = []
                for _hi in range(6):
                    _b = (6 * _k + _hi) % 3
                    _bu[_b] += 1
                    row.append((_b, _bu[_b]))
                evac_cnt.append(row)

            with nc.Block("p23", no_gpsimd_drain=True) as blk:

                @blk.gpsimd
                def _(g):
                    for w_src, w_sb in ((wqt_in, wqT), (wkt_in, wkT),
                                        (wvt_in, wvT)):
                        for c in range(8):
                            i = g.dma_start(out=w_sb[:, D * c:D * (c + 1)],
                                            in_=w_src[128 * c:128 * (c + 1), :])
                            i.then_inc(s_wq, 16)
                    for k in range(NT + DEPTH23):
                        if k < NT:
                            if k >= 2:
                                g.wait_ge(s_pe2, 6 * (k - 1))
                            i = g.dma_start(
                                out=xT_t[:, D * (k % 2):D * (k % 2 + 1)].rearrange(
                                    "p (c t) -> p c t", c=8),
                                in_=AP(xT_in.tensor, 128 * k,
                                       [[T, 128], [128 * T, 8], [1, 128]]))
                            i.then_inc(s_xt[k % 2], 16)
                        if k >= 1 and k - 1 < NT:
                            j = k - 1
                            par = j % 2
                            o2 = D * par
                            g.wait_ge(s_dve2, j + 1)
                            for _hi in (4, 5):
                                _b, _c = evac_cnt[j][_hi]
                                g.wait_ge(s_ev2[_b], _c)
                            if j >= 2:
                                g.wait_ge(s_sto2g[par], 32 * (j // 2))
                            r0, r1 = 128 * j, 128 * (j + 1)
                            i = g.dma_start(out=ek_d[r0:r1, :],
                                            in_=ekb_t[:, o2:o2 + D])
                            i.then_inc(s_sto2g[par], 16)
                            i = g.dma_start(out=v_d[r0:r1, :],
                                            in_=vb_t[:, o2:o2 + D])
                            i.then_inc(s_sto2g[par], 16)
                        if k >= 4 and k - 4 < NT:
                            j = k - 4
                            par = j % 2
                            base = 128 * j
                            g.wait_ge(s_m1ev, j + 1)
                            if j >= 2:
                                g.wait_ge(s_wtst[par], 16 * (j // 2))
                            dst = AP(wt_d.tensor, base * 256,
                                     [[512, 4], [256, 2], [16, 16],
                                      [8 * 256, 16], [1, 16]])
                            i = g.dma_start(out=dst,
                                            in_=wsb[:, 256 * par:256 * (par + 1)]
                                            .rearrange("p (q h) -> p q h", h=16))
                            i.then_inc(s_wtst[par], 16)
                        if k >= 7 and k - 7 < NT:
                            j = k - 7
                            par = j % 2
                            g.wait_ge(s_m2ev, 2 * (j + 1))
                            if j >= 2:
                                g.wait_ge(s_attst[par], 32 * (j // 2))
                            for hb in range(2):
                                dst = AP(att_d.tensor, (128 * j + 64 * hb) * D,
                                         [[D, 8], [64, 16], [8 * D, 8], [1, 64]])
                                i = g.dma_start(
                                    out=dst,
                                    in_=asb[:, 1024 * par + 512 * hb:
                                            1024 * par + 512 * (hb + 1)]
                                    .rearrange("p (a e) -> p a e", e=64))
                                i.then_inc(s_attst[par], 16)

                @blk.sync
                def _(sy):
                    sy.wait_ge(s_ms3, 1)
                    for k in range(NT + DEPTH23):
                        if k >= 1 and k - 1 < NT:
                            j = k - 1
                            par = j % 2
                            o = D * par
                            sy.wait_ge(s_dve2, j + 1)
                            if j >= 2:
                                sy.wait_ge(s_sto2[par], 16 * (j // 2))
                            r0, r1 = 128 * j, 128 * (j + 1)
                            i = sy.dma_start(out=eqp_d[r0:r1, :],
                                             in_=eqp_t[:, o:o + D])
                            i.then_inc(s_sto2[par], 16)
                        if k >= 2 and k - 2 < NT:
                            j = k - 2
                            par = j % 2
                            base = 128 * j
                            sy.wait_ge(s_sto2[par], 16 * (j // 2 + 1))
                            sy.wait_ge(s_sto2g[par], 32 * (j // 2 + 1))
                            if j >= 2:
                                sy.wait_ge(s_m1, j - 1)
                            dst_e = bd1[0:64, 2048 * par:2048 * (par + 1)].rearrange(
                                "p (a c) -> p a c", c=32)[:, :, 0:16]
                            i = sy.dma_start(out=dst_e,
                                             in_=AP(ek_d.tensor, base * D,
                                                    [[16, 64], [2 * D, 64], [1, 16]]))
                            i.then_inc(s_ld3[par], 16)
                            dst_o = bd1[64:128, 2048 * par:2048 * (par + 1)].rearrange(
                                "p (a c) -> p a c", c=32)[:, :, 16:32]
                            i = sy.dma_start(out=dst_o,
                                             in_=AP(ek_d.tensor, (base + 1) * D,
                                                    [[16, 64], [2 * D, 64], [1, 16]]))
                            i.then_inc(s_ld3[par], 16)
                            i = sy.dma_start(
                                out=eq1[:, 1024 * par:1024 * (par + 1)].rearrange(
                                    "p (a h) -> p a h", h=16),
                                in_=AP(eqp_d.tensor, base * D,
                                       [[1024, 2], [16, 64], [2048, 64], [1, 16]]))
                            i.then_inc(s_ld3[par], 16)

                @blk.scalar
                def _(s):
                    s.wait_ge(s_ms3, 1)
                    for k in range(NT + DEPTH23):
                        if k < NT:
                            o = D * (k % 2)
                            if k >= 2:
                                s.wait_ge(s_dve2, k - 1)
                                s.wait_ge(s_sto2g[k % 2], 32 * (k // 2))
                            outs = (eq32[:, o:o + 512], eq32[:, o + 512:o + D],
                                    ek32[:, o:o + 512], ek32[:, o + 512:o + D],
                                    vb_t[:, o:o + 512], vb_t[:, o + 512:o + D])
                            for hi in range(6):
                                b, _c = evac_cnt[k][hi]
                                s.wait_ge(s_pe2, 6 * k + hi + 1)
                                if hi < 4:
                                    i = s.activation(outs[hi], PB[b][:, :], AF.Exp)
                                else:
                                    i = s.copy(outs[hi], PB[b][:, :])
                                i.then_inc(s_ev2[b], 1)
                        if k >= 5 and k - 5 < NT:
                            j = k - 5
                            par = j % 2
                            base = 128 * j
                            if j >= 2:
                                s.wait_ge(s_m2, j - 1)
                            s.wait_ge(s_wtst[par], 16 * (j // 2 + 1))
                            s.wait_ge(s_sto2g[par], 32 * (j // 2 + 1))
                            for j2 in range(8):
                                dst = bd2[16 * j2:16 * (j2 + 1),
                                          2048 * par:2048 * (par + 1)].rearrange(
                                    "p (a b) -> p a b", b=128)[:, :,
                                                              16 * j2:16 * (j2 + 1)]
                                i = s.dma_start(
                                    out=dst,
                                    in_=AP(wt_d.tensor, (base + j2) * 256,
                                           [[16, 16], [8 * 256, 16], [1, 16]]))
                                i.then_inc(s_ld3b[par], 16)
                            i = s.dma_start(
                                out=vs2[:, 1024 * par:1024 * (par + 1)].rearrange(
                                    "p (a e) -> p a e", e=64),
                                in_=AP(v_d.tensor, base * D,
                                       [[D, 8], [64, 16], [8 * D, 16], [1, 64]]))
                            i.then_inc(s_ld3b[par], 16)

                @blk.tensor
                def _(t):
                    t.wait_ge(s_wq, 24 * 16)
                    pe_bank_uses = [0, 0, 0]
                    for k in range(NT + DEPTH23):
                        if k < NT:
                            t.wait_ge(s_xt[k % 2], 16 * (k // 2 + 1))
                            for mi, wT in enumerate((wqT, wkT, wvT)):
                                for hf in range(2):
                                    hi = 2 * mi + hf
                                    b = (6 * k + hi) % 3
                                    if pe_bank_uses[b] > 0:
                                        t.wait_ge(s_ev2[b], pe_bank_uses[b])
                                    pe_bank_uses[b] += 1
                                    for c in range(8):
                                        lhsT = xT_t[:, D * (k % 2) + 128 * c:
                                                    D * (k % 2) + 128 * (c + 1)]
                                        i = t.matmul(PB[b][:, :], lhsT,
                                                     wT[:, D * c + 512 * hf:
                                                        D * c + 512 * (hf + 1)],
                                                     start=(c == 0), stop=(c == 7))
                                    i.then_inc(s_pe2, 1)
                        if k >= 3 and k - 3 < NT:
                            j = k - 3
                            par = j % 2
                            t.wait_ge(s_ld3[par], 48 * (j // 2 + 1))
                            if j >= 2:
                                t.wait_ge(s_m1ev, j - 1)
                            for jj in range(64):
                                c, q = jj % 4, jj // 4
                                i = t.matmul(
                                    PB[3 + par][32 * c:32 * c + 32,
                                                16 * q:16 * (q + 1)],
                                    bd1[:, 2048 * par + 32 * jj:
                                        2048 * par + 32 * (jj + 1)],
                                    eq1[:, 1024 * par + 16 * jj:
                                        1024 * par + 16 * (jj + 1)],
                                    start=True, stop=True,
                                    tile_position=(0, 32 * c))
                            i.then_inc(s_m1, 1)
                        if k >= 6 and k - 6 < NT:
                            j = k - 6
                            par = j % 2
                            t.wait_ge(s_ld3b[par], 144 * (j // 2 + 1))
                            t.wait_ge(s_m2ev, 2 * j)
                            for grp in range(16):
                                bank = PB[5 + grp // 8]
                                i = t.matmul(
                                    bank[:, 64 * (grp % 8):64 * (grp % 8 + 1)],
                                    bd2[:, 2048 * par + 128 * grp:
                                        2048 * par + 128 * (grp + 1)],
                                    vs2[:, 1024 * par + 64 * grp:
                                        1024 * par + 64 * (grp + 1)],
                                    start=True, stop=True)
                            i.then_inc(s_m2, 1)

                @blk.vector
                def _(v):
                    i = v.memset(bd1[:, :], 0.0)
                    i = v.memset(bd2[:, :], 0.0)
                    i.then_inc(s_ms3, 1)
                    for k in range(NT + DEPTH23):
                        if k < NT:
                            o = D * (k % 2)
                            oh = HD * (k % 2)
                            if k >= 2:
                                v.wait_ge(s_sto2[k % 2], 16 * (k // 2))
                                v.wait_ge(s_sto2g[k % 2], 32 * (k // 2))
                            for _hi in range(4):
                                _b, _c = evac_cnt[k][_hi]
                                v.wait_ge(s_ev2[_b], _c)
                            i = v.reduce_sum(sd_t[:, oh:oh + HD],
                                             ek32[:, o:o + D].rearrange(
                                                 "p (d h) -> p d h", h=H),
                                             axis=AX.X)
                            chain_v(v, i)
                            i = v.reciprocal(sd_t[:, oh:oh + HD],
                                             sd_t[:, oh:oh + HD])
                            chain_v(v, i)
                            v.tensor_tensor(
                                eqp_t[:, o:o + D].rearrange("p (d h) -> p d h",
                                                            h=H),
                                eq32[:, o:o + D].rearrange("p (d h) -> p d h",
                                                           h=H),
                                sd_t[:, oh:oh + HD].rearrange(
                                    "p (d u) -> p d u", u=1).broadcast_to(
                                    [128, HD, H]),
                                ALU.mult)
                            v.reduce_sum(dq_all[:, H * k:H * (k + 1)],
                                         eq32[:, o:o + D].rearrange(
                                             "p (d h) -> p h d", h=H),
                                         axis=AX.X)
                            i = v.tensor_copy(ekb_t[:, o:o + D],
                                              ek32[:, o:o + D])
                            i.then_inc(s_dve2, 1)
                        if k >= 3 and k - 3 < NT:
                            j = k - 3
                            par = j % 2
                            v.wait_ge(s_m1, j + 1)
                            if j >= 2:
                                v.wait_ge(s_wtst[par], 16 * (j // 2))
                            i = v.tensor_copy(wsb[:, 256 * par:256 * (par + 1)],
                                              PB[3 + par][:, 0:256])
                            i.then_inc(s_m1ev, 1)
                        if k >= 6 and k - 6 < NT:
                            j = k - 6
                            par = j % 2
                            v.wait_ge(s_m2, j + 1)
                            if j >= 2:
                                v.wait_ge(s_attst[par], 32 * (j // 2))
                            v.tensor_copy(asb[:, 1024 * par:1024 * par + 512],
                                          PB[5][:, :])
                            i = v.tensor_copy(asb[:, 1024 * par + 512:
                                                  1024 * par + 1024],
                                              PB[6][:, :])
                            i.then_inc(s_m2ev, 2)

        if BUILD_STAGE >= 4:
            ctx23.close()

            ctx45 = contextlib.ExitStack()

            def sbuf45(name, shape, dtype):
                return ctx45.enter_context(nc.sbuf_tensor(name, shape, dtype))

            xf_t = sbuf45("xf_t", [128, 2 * D], F32)
            ab_t = sbuf45("ab_t", [128, 2 * D], BF16)
            s1_t = sbuf45("s1_t", [128, 2 * D], F32)
            y32t = sbuf45("y32t", [128, 2 * D], F32)
            ybf_t = sbuf45("ybf_t", [128, 2 * D], BF16)
            yT_t = sbuf45("yT_t", [128, 2 * 8 * D], BF16)
            lgb = sbuf45("lgb", [128, 2 * 64], F32)
            lmb = sbuf45("lmb", [128, 2 * 64], F32)
            exb = sbuf45("exb", [128, 2 * 64], F32)
            c64 = sbuf45("c64", [128, 2 * 64], F32)
            em_t = sbuf45("em_t", [128, 2 * 8], F32)
            cum_t = sbuf45("cum_t", [128, 2 * 8], F32)
            ex8_t = sbuf45("ex8_t", [128, 2 * 8], F32)
            sc4 = sbuf45("sc4", [128, 32], F32)
            sq_t = sbuf45("sq_t", [128, D], F32)

            with nc.Block("p45", no_gpsimd_drain=True) as blk:

                @blk.sync
                def _(sy):
                    sy.wait_ge(s_attst[0], 32 * 16)
                    sy.wait_ge(s_attst[1], 32 * 16)
                    for k in range(NT + 2):
                        par = k % 2
                        if k < NT:
                            if k >= 2:
                                sy.wait_ge(s_y, k - 1)
                            i = sy.dma_start(out=xf_t[:, D * par:D * (par + 1)],
                                             in_=x_in[128 * k:128 * (k + 1), :])
                            i.then_inc(s_ld4[par], 16)
                            i = sy.dma_start(out=ab_t[:, D * par:D * (par + 1)],
                                             in_=att_d[128 * k:128 * (k + 1), :])
                            i.then_inc(s_ld4[par], 16)
                        if k >= 1 and k - 1 < NT:
                            kp = k - 1
                            pp = kp % 2
                            sy.wait_ge(s_y, kp + 1)
                            if kp >= 2:
                                sy.wait_ge(s_yst[pp], 32 * (kp // 2))
                            i = sy.dma_start(out=y32_d[128 * kp:128 * (kp + 1), :],
                                             in_=y32t[:, D * pp:D * (pp + 1)])
                            i.then_inc(s_yst[pp], 16)
                            i = sy.dma_start(out=ybf_d[128 * kp:128 * (kp + 1), :],
                                             in_=ybf_t[:, D * pp:D * (pp + 1)])
                            i.then_inc(s_yst[pp], 16)
                        if k >= 8 and (k - 8) % 8 == 0:
                            G = (k - 8) // 8
                            gp_ = G % 2
                            sy.wait_ge(s_yst[0], 32 * 4 * (G + 1))
                            sy.wait_ge(s_yst[1], 32 * 4 * (G + 1))
                            if G >= 2:
                                sy.wait_ge(s_gate, 8 * (G - 1))
                            for c in range(8):
                                i = sy.dma_start(
                                    out=yT_t[:, 8 * D * gp_ + 1024 * c:
                                             8 * D * gp_ + 1024 * (c + 1)],
                                    in_=ybf_d[1024 * G:1024 * (G + 1),
                                              128 * c:128 * (c + 1)],
                                    transpose=True)
                                i.then_inc(s_yT[gp_], 16)

                @blk.vector
                def _(v):
                    for k in range(NT + 10):
                        par = k % 2
                        o = D * par
                        so = 16 * par
                        if k < NT:
                            v.wait_ge(s_ld4[par], 32 * (k // 2 + 1))
                            if k >= 2:
                                v.wait_ge(s_yst[par], 32 * (k // 2))
                                v.wait_ge(s_rstd, k - 1)
                            i = v.reciprocal(sc4[:, so:so + 16],
                                             dq_all[:, H * k:H * (k + 1)])
                            chain_v(v, i)
                            i = v.tensor_tensor(
                                s1_t[:, o:o + D].rearrange("p (h e) -> p h e", e=HD),
                                ab_t[:, o:o + D].rearrange("p (h e) -> p h e", e=HD),
                                sc4[:, so:so + 16].rearrange(
                                    "p (h u) -> p h u", u=1).broadcast_to(
                                    [128, H, HD]),
                                ALU.mult)
                            chain_v(v, i)
                            i = v.tensor_add(s1_t[:, o:o + D], s1_t[:, o:o + D],
                                             xf_t[:, o:o + D])
                            i.then_inc(s_va, 1)
                            v.wait_ge(s_va, 3 * k + 1)
                            i = v.reduce_sum(sc4[:, so + 8:so + 9],
                                            s1_t[:, o:o + D], axis=AX.X)
                            chain_v(v, i)
                            i = v.tensor_scalar_mul(sc4[:, so + 9:so + 10],
                                                    sc4[:, so + 8:so + 9],
                                                    1.0 / D)
                            i.then_inc(s_va, 1)
                            v.wait_ge(s_va, 3 * k + 2)
                            i = v.tensor_mul(sc4[:, so + 13:so + 14],
                                             sc4[:, so + 9:so + 10],
                                             sc4[:, so + 9:so + 10])
                            chain_v(v, i)
                            v.wait_ge(s_ssq, k + 1)
                            i = v.scalar_tensor_tensor(sc4[:, so + 14:so + 15],
                                                       sc4[:, so + 15:so + 16],
                                                       1.0 / D,
                                                       sc4[:, so + 13:so + 14],
                                                       ALU.mult, ALU.subtract)
                            i.then_inc(s_va, 1)
                            v.wait_ge(s_rstd, k + 1)
                            i = v.scalar_tensor_tensor(
                                y32t[:, o:o + D], s1_t[:, o:o + D],
                                sc4[:, so + 9:so + 10],
                                sc4[:, so + 10:so + 11].broadcast_to([128, D]),
                                ALU.subtract, ALU.mult)
                            i.then_inc(s_ap, 1)
                        if k >= 10 and k - 10 < NT:
                            kk = k - 10
                            pp = kk % 2
                            if kk % 8 == 0:
                                G = kk // 8
                                gp_ = G % 2
                                go = 64 * gp_
                                v.wait_ge(s_gate, 8 * (G + 1))
                                if G >= 2:
                                    v.wait_ge(s_em, G - 1)
                                    v.wait_ge(s_pa, G - 1)
                                i = v.tensor_copy(lgb[:, go:go + 64],
                                                  PB[6 + gp_][:, 0:64])
                                i.then_inc(s_gev, 1)
                                v.wait_ge(s_gev, G + 1)
                                i = v.reduce_max(em_t[:, 8 * gp_:8 * (gp_ + 1)],
                                                 lgb[:, go:go + 64].rearrange(
                                                     "p (g e) -> p g e", e=8),
                                                 axis=AX.X)
                                chain_v(v, i)
                                m8b = em_t[:, 8 * gp_:8 * (gp_ + 1)].rearrange(
                                    "p (g u) -> p g u", u=1).broadcast_to(
                                    [128, 8, 8])
                                i = v.tensor_tensor(
                                    lmb[:, go:go + 64].rearrange(
                                        "p (g e) -> p g e", e=8),
                                    lgb[:, go:go + 64].rearrange(
                                        "p (g e) -> p g e", e=8),
                                    m8b, ALU.subtract)
                                i.then_inc(s_lm, 1)
                                i = v.tensor_tensor(
                                    c64[:, go:go + 64].rearrange(
                                        "p (g e) -> p g e", e=8),
                                    lgb[:, go:go + 64].rearrange(
                                        "p (g e) -> p g e", e=8),
                                    m8b, ALU.is_equal)
                                chain_v(v, i)
                                i = v.tensor_scalar(c64[:, go:go + 64],
                                                    c64[:, go:go + 64],
                                                    -BIG, BIG,
                                                    ALU.mult, ALU.add)
                                chain_v(v, i)
                                i = v.tensor_add(c64[:, go:go + 64],
                                                 c64[:, go:go + 64],
                                                 iota64[:, :])
                                chain_v(v, i)
                                i = v.tensor_reduce(
                                    et_all[:, 8 * G:8 * (G + 1)],
                                    c64[:, go:go + 64].rearrange(
                                        "p (g e) -> p g e", e=8),
                                    AX.X, ALU.min)
                                chain_v(v, i)
                                etb = et_all[:, 8 * G:8 * (G + 1)].rearrange(
                                    "p (g u) -> p g u", u=1).broadcast_to(
                                    [128, 8, 8])
                                i = v.tensor_tensor(
                                    oh_all[:, 64 * G:64 * (G + 1)].rearrange(
                                        "p (g e) -> p g e", e=8),
                                    iota64[:, :].rearrange(
                                        "p (g e) -> p g e", e=8),
                                    etb, ALU.is_equal)
                                i.then_inc(s_ohg, 1)
                                v.wait_ge(s_em, G + 1)
                                i = v.reduce_sum(cum_t[:, 8 * gp_:8 * (gp_ + 1)],
                                                 exb[:, go:go + 64].rearrange(
                                                     "p (g e) -> p g e", e=8),
                                                 axis=AX.X)
                                chain_v(v, i)
                                i = v.reciprocal(pmax_all[:, 8 * G:8 * (G + 1)],
                                                 cum_t[:, 8 * gp_:8 * (gp_ + 1)])
                                i.then_inc(s_pa, 1)
                            v.wait_ge(s_5p, kk + 1)
                            i = v.tensor_mul(ex8_t[:, 8 * pp:8 * (pp + 1)],
                                             oh_all[:, 8 * kk:8 * (kk + 1)],
                                             PB[4 + pp][:, 0:8])
                            chain_v(v, i)
                            i = v.reduce_sum(lpos_all[:, kk:kk + 1],
                                             ex8_t[:, 8 * pp:8 * (pp + 1)],
                                             axis=AX.X)
                            chain_v(v, i)
                            i = v.tensor_scalar_add(lpos_all[:, kk:kk + 1],
                                                    lpos_all[:, kk:kk + 1], -1.0)
                            chain_v(v, i)
                            i = v.tensor_add(carry_sb[:, :], carry_sb[:, :],
                                             PB[4 + pp][:, 8:16])
                            i.then_inc(s_5r, 1)
                    v.wait_ge(s_5r, NT)
                    i = v.tensor_copy(cnt_sb[0:1, 0:8], carry_sb[0:1, 0:8])
                    i.then_inc(s_cnt, 1)

                @blk.gpsimd
                def _(g):
                    g.wait_ge(s_ms, 1)
                    for r in range(NSLOT // 128):
                        i = g.dma_start(out=expin_d[128 * r:128 * (r + 1), :],
                                        in_=zer_bf[:, :])
                        i.then_inc(s_zin, 16)
                    i = g.dma_start(out=agout_d[NSLOT:NSLOT + 128, :],
                                    in_=zer_bf[:, :])
                    i.then_inc(s_zin, 16)

                @blk.scalar
                def _(s):
                    for k in range(NT + 10):
                        par = k % 2
                        o = D * par
                        so = 16 * par
                        if k < NT:
                            s.wait_ge(s_va, 3 * k + 1)
                            i = s.activation(sq_t[:, :], s1_t[:, o:o + D],
                                             AF.Square,
                                             accum_out=sc4[:, so + 15:so + 16])
                            i.then_inc(s_ssq, 1)
                            s.wait_ge(s_va, 3 * k + 3)
                            i = s.activation(sc4[:, so + 14:so + 15],
                                             sc4[:, so + 14:so + 15], AF.Ln,
                                             bias=eps_t[:, :])
                            chain_a(s, i)
                            i = s.activation(sc4[:, so + 10:so + 11],
                                             sc4[:, so + 14:so + 15], AF.Exp,
                                             scale=-0.5)
                            i.then_inc(s_rstd, 1)
                            s.wait_ge(s_ap, k + 1)
                            if k >= 2:
                                s.wait_ge(s_yst[k % 2], 32 * (k // 2))
                            i = s.copy(ybf_t[:, o:o + D], y32t[:, o:o + D])
                            i.then_inc(s_y, 1)
                        if k >= 10 and k - 10 < NT and (k - 10) % 8 == 0:
                            G = (k - 10) // 8
                            gp_ = G % 2
                            go = 64 * gp_
                            s.wait_ge(s_lm, G + 1)
                            if G >= 2:
                                s.wait_ge(s_pa, G - 1)
                            i = s.activation(exb[:, go:go + 64],
                                             lmb[:, go:go + 64], AF.Exp)
                            i.then_inc(s_em, 1)

                @blk.tensor
                def _(t):
                    for k in range(NT + 10):
                        if k >= 9 and (k - 9) % 8 == 0 and (k - 9) // 8 < NT // 8:
                            G = (k - 9) // 8
                            gp_ = G % 2
                            t.wait_ge(s_yT[gp_], 8 * 16 * (G // 2 + 1))
                            if G >= 2:
                                t.wait_ge(s_gev, G - 1)
                            for j in range(8):
                                kk = 8 * G + j
                                for c in range(8):
                                    i = t.matmul(
                                        PB[6 + gp_][:, 8 * j:8 * (j + 1)],
                                        yT_t[:, 8 * D * gp_ + 1024 * c + 128 * j:
                                             8 * D * gp_ + 1024 * c + 128 * (j + 1)],
                                        gwc[:, 8 * c:8 * (c + 1)],
                                        start=(c == 0), stop=(c == 7))
                                i.then_inc(s_gate, 1)
                            t.wait_ge(s_ohg, G + 1)
                            for j in range(8):
                                kk = 8 * G + j
                                pp = kk % 2
                                if kk >= 2:
                                    t.wait_ge(s_5r, kk - 1)
                                t.matmul(PB[4 + pp][:, 0:8], l2tri[:, :],
                                         oh_all[:, 8 * kk:8 * (kk + 1)],
                                         start=True, stop=False)
                                t.wait_ge(s_5r, kk)
                                t.matmul(PB[4 + pp][:, 0:8], ones128[0:1, :],
                                         carry_sb[0:1, 0:8],
                                         start=False, stop=True)
                                i = t.matmul(PB[4 + pp][:, 8:16], ones128[:, :],
                                             oh_all[:, 8 * kk:8 * (kk + 1)],
                                             start=True, stop=True)
                                i.then_inc(s_5p, 1)

        if BUILD_STAGE >= 4:
            ctx45.close()

            ctx67 = contextlib.ExitStack()

            def sbuf67(name, shape, dtype):
                return ctx67.enter_context(nc.sbuf_tensor(name, shape, dtype))

            em5 = sbuf67("em5", [128, 8], F32)
            sc5 = sbuf67("sc5", [128, 8], F32)
            ysc_t = sbuf67("ysc_t", [128, 2 * D], BF16)
            if BUILD_STAGE >= 9:
                w1big = sbuf67("w1big", [128, 8 * FF], BF16)
                w2big = sbuf67("w2big", [128, 32 * D], BF16)
                yTs = sbuf67("yTs", [128, 2 * 8 * 512], BF16)
                hT = sbuf67("hT", [128, 32 * 512], BF16)
                eo_t = sbuf67("eo_t", [128, 2 * D], BF16)

            with nc.Block("p5c", no_gpsimd_drain=True) as blk:

                @blk.gpsimd
                def _(g):
                    i = g.dma_start(out=cnt_in_d[:, :], in_=cnt_sb[0:1, 0:8])
                    i.then_inc(s_cc1, 16)
                    g.wait_ge(s_cc1, 16)
                    i = g.collective_compute(
                        "AllGather", ALU.bypass,
                        replica_groups=[list(range(NCORES))],
                        ins=[cnt_in_d], outs=[cnt_ag_d])
                    i.then_inc(s_cc1b, 1)
                    g.wait_ge(s_cc1b, 1)
                    i = g.dma_start(out=cnt_sb[0:8, 0:8], in_=cnt_ag_d[:, :])
                    i.then_inc(s_cc1, 16)

                @blk.tensor
                def _(t):
                    t.wait_ge(s_cc1, 32)
                    i = t.matmul(PB[4][:, 0:8], maskB[0:8, :], cnt_sb[0:8, 0:8],
                                 start=True, stop=True)
                    i.then_inc(s_bm, 1)

                @blk.vector
                def _(v):
                    v.wait_ge(s_bm, 1)
                    v.tensor_copy(b_sb[:, :], PB[4][:, 0:8])

            with nc.Block("p5d", no_gpsimd_drain=True) as blk:

                @blk.vector
                def _(v):
                    for kk in range(NT):
                        i = v.tensor_mul(em5[:, 0:8], oh_all[:, 8 * kk:8 * (kk + 1)],
                                         b_sb[:, :])
                        chain_v(v, i)
                        i = v.reduce_sum(sc5[:, 0:1], em5[:, 0:8], axis=AX.X)
                        chain_v(v, i)
                        i = v.tensor_add(sc5[:, 0:1], sc5[:, 0:1],
                                         lpos_all[:, kk:kk + 1])
                        chain_v(v, i)
                        i = v.tensor_single_scalar(sc5[:, 1:2], sc5[:, 0:1],
                                                   float(CAP), ALU.is_ge)
                        chain_v(v, i)
                        i = v.scalar_tensor_tensor(sc5[:, 2:3],
                                                   et_all[:, kk:kk + 1],
                                                   float(CAP),
                                                   sc5[:, 0:1],
                                                   ALU.mult, ALU.add)
                        chain_v(v, i)
                        i = v.scalar_tensor_tensor(sc5[:, 3:4],
                                                   sc5[:, 1:2], BIG,
                                                   sc5[:, 2:3],
                                                   ALU.mult, ALU.add)
                        chain_v(v, i)
                        i = v.tensor_single_scalar(sc5[:, 3:4], sc5[:, 3:4],
                                                   float(NSLOT), ALU.min)
                        chain_v(v, i)
                        i = v.tensor_copy(slot_all[:, kk:kk + 1], sc5[:, 3:4])
                        i.then_inc(s_5b, 1)

                @blk.sync
                def _(sy):
                    for kk in range(NT):
                        par = kk % 2
                        if kk >= 2:
                            sy.wait_ge(s_sc6[par], 16 * (kk // 2))
                        i = sy.dma_start(out=ysc_t[:, D * par:D * (par + 1)],
                                         in_=ybf_d[128 * kk:128 * (kk + 1), :])
                        i.then_inc(s_ld6[par], 16)

                @blk.gpsimd
                def _(g):
                    g.wait_ge(s_zin, 16 * (NSLOT // 128 + 1))
                    for kk in range(NT):
                        par = kk % 2
                        g.wait_ge(s_5b, kk + 1)
                        g.wait_ge(s_ld6[par], 16 * (kk // 2 + 1))
                        i = g.indirect_dma_start(
                            out=expin_d[:, :],
                            out_offset=bass.IndirectOffsetOnAxis(
                                ap=slot_all[:, kk:kk + 1], axis=0),
                            in_=ysc_t[:, D * par:D * (par + 1)],
                            in_offset=None,
                            bounds_check=NSLOT - 1, oob_is_err=False)
                        i.then_inc(s_sc6[par], 16)
                    if BUILD_STAGE >= 9:
                        for c in range(8):
                            i = g.dma_start(out=w1big[:, FF * c:FF * (c + 1)],
                                            in_=w1_in[128 * c:128 * (c + 1), :])
                            i.then_inc(s_w1l, 16)
                        for f in range(32):
                            i = g.dma_start(out=w2big[:, D * f:D * (f + 1)],
                                            in_=w2_in[128 * f:128 * (f + 1), :])
                            i.then_inc(s_w1l, 16)
                    g.wait_ge(s_sc6[0], 16 * (NT // 2))
                    g.wait_ge(s_sc6[1], 16 * (NT // 2))
                    i = g.collective_compute(
                        "ReduceScatter", ALU.add,
                        replica_groups=[list(range(NCORES))],
                        ins=[expin_d], outs=[rsout_d])
                    i.then_inc(s_cc2, 1)

        if BUILD_STAGE >= 9:
            with nc.Block("p7", no_gpsimd_drain=True) as blk:
                MSZ = [512, 512, 256]

                @blk.sync
                def _(sy):
                    sy.wait_ge(s_cc2, 1)
                    for m in range(3):
                        msz = MSZ[m]
                        par = m % 2
                        if m >= 2:
                            sy.wait_ge(s_mm1, 32 * (m - 1))
                        for c in range(8):
                            i = sy.dma_start(
                                out=yTs[:, 4096 * par + 512 * c:
                                        4096 * par + 512 * c + msz],
                                in_=rsout_d[512 * m:512 * m + msz,
                                            128 * c:128 * (c + 1)],
                                transpose=True)
                            i.then_inc(s_yTs[par], 16)

                @blk.tensor
                def _(t):
                    t.wait_ge(s_w1l, 40 * 16)
                    stg = 0
                    for m in range(3):
                        msz = MSZ[m]
                        par = m % 2
                        t.wait_ge(s_yTs[par], 8 * 16 * (m // 2 + 1))
                        for f in range(32):
                            fi = 32 * m + f
                            if fi >= 2:
                                t.wait_ge(s_h1, fi - 1)
                            for d_ in range(8):
                                i = t.matmul(
                                    PB[f % 2][:, 0:msz],
                                    w1big[:, FF * d_ + 128 * f:
                                          FF * d_ + 128 * (f + 1)],
                                    yTs[:, 4096 * par + 512 * d_:
                                        4096 * par + 512 * d_ + msz],
                                    start=(d_ == 0), stop=(d_ == 7))
                            i.then_inc(s_mm1, 1)
                        t.wait_ge(s_h1, 32 * (m + 1))
                        for st in range(msz // 128):
                            pp2 = stg % 2
                            bk = (PB[2 + 2 * pp2], PB[3 + 2 * pp2])
                            if stg >= 2:
                                t.wait_ge(s_eoev, 2 * (stg - 1))
                            for f in range(32):
                                for hf in range(2):
                                    i = t.matmul(
                                        bk[hf][:, :],
                                        hT[:, 512 * f + 128 * st:
                                           512 * f + 128 * (st + 1)],
                                        w2big[:, D * f + 512 * hf:
                                              D * f + 512 * (hf + 1)],
                                        start=(f == 0), stop=(f == 31))
                            i.then_inc(s_mm2, 1)
                            stg += 1

                @blk.scalar
                def _(s):
                    for m in range(3):
                        msz = MSZ[m]
                        for f in range(32):
                            fi = 32 * m + f
                            s.wait_ge(s_mm1, fi + 1)
                            i = s.activation(hT[:, 512 * f:512 * f + msz],
                                             PB[f % 2][:, 0:msz], AF.Relu)
                            i.then_inc(s_h1, 1)

                @blk.vector
                def _(v):
                    for stg in range(10):
                        pp2 = stg % 2
                        v.wait_ge(s_mm2, stg + 1)
                        if stg >= 2:
                            v.wait_ge(s_eost[pp2], 16 * (stg // 2))
                        v.tensor_copy(eo_t[:, D * pp2:D * pp2 + 512],
                                      PB[2 + 2 * pp2][:, :])
                        i = v.tensor_copy(eo_t[:, D * pp2 + 512:D * (pp2 + 1)],
                                          PB[3 + 2 * pp2][:, :])
                        i.then_inc(s_eoev, 2)

                @blk.gpsimd
                def _(g):
                    for stg in range(10):
                        pp2 = stg % 2
                        g.wait_ge(s_eoev, 2 * (stg + 1))
                        if stg >= 2:
                            g.wait_ge(s_eost[pp2], 16 * (stg // 2))
                        i = g.dma_start(out=eout_d[128 * stg:128 * (stg + 1), :],
                                        in_=eo_t[:, D * pp2:D * (pp2 + 1)])
                        i.then_inc(s_eost[pp2], 16)
                    g.wait_ge(s_eost[0], 16 * 5)
                    g.wait_ge(s_eost[1], 16 * 5)
                    i = g.collective_compute(
                        "AllGather", ALU.bypass,
                        replica_groups=[list(range(NCORES))],
                        ins=[eout_d], outs=[agout_d[0:NSLOT, :]])
                    i.then_inc(s_cc3, 1)

            ctx67.close()

            ctx9 = contextlib.ExitStack()

            def sbuf9(name, shape, dtype):
                return ctx9.enter_context(nc.sbuf_tensor(name, shape, dtype))

            gat_t = sbuf9("gat_t", [128, 2 * D], BF16)
            yf9_t = sbuf9("yf9_t", [128, 2 * D], F32)
            o9_t = sbuf9("o9_t", [128, 2 * D], F32)
            sq9 = sbuf9("sq9", [128, D], F32)
            sc9 = sbuf9("sc9", [128, 32], F32)

            with nc.Block("p9", no_gpsimd_drain=True) as blk:

                @blk.gpsimd
                def _(g):
                    g.wait_ge(s_cc3, 1)
                    for kk in range(NT):
                        par = kk % 2
                        if kk >= 2:
                            g.wait_ge(s_o9, kk - 1)
                        i = g.indirect_dma_start(
                            out=gat_t[:, D * par:D * (par + 1)],
                            out_offset=None,
                            in_=agout_d[:, :],
                            in_offset=bass.IndirectOffsetOnAxis(
                                ap=slot_all[:, kk:kk + 1], axis=0))
                        i.then_inc(s_g9[par], 16)

                @blk.sync
                def _(sy):
                    for kk in range(NT + 1):
                        par = kk % 2
                        if kk < NT:
                            if kk >= 2:
                                sy.wait_ge(s_o9, kk - 1)
                            i = sy.dma_start(
                                out=yf9_t[:, D * par:D * (par + 1)],
                                in_=y32_d[128 * kk:128 * (kk + 1), :])
                            i.then_inc(s_ld9[par], 16)
                        if kk >= 1:
                            kp = kk - 1
                            pp = kp % 2
                            sy.wait_ge(s_o9, kp + 1)
                            if kp >= 2:
                                sy.wait_ge(s_ost9[pp], 16 * (kp // 2))
                            i = sy.dma_start(
                                out=out_t[128 * kp:128 * (kp + 1), :],
                                in_=o9_t[:, D * pp:D * (pp + 1)])
                            i.then_inc(s_ost9[pp], 16)

                @blk.vector
                def _(v):
                    for kk in range(NT):
                        par = kk % 2
                        so = 16 * par
                        v.wait_ge(s_g9[par], 16 * (kk // 2 + 1))
                        v.wait_ge(s_ld9[par], 16 * (kk // 2 + 1))
                        if kk >= 2:
                            v.wait_ge(s_ost9[par], 16 * (kk // 2))
                            v.wait_ge(s_rstd9, kk - 1)
                        i = v.tensor_scalar(o9_t[:, D * par:D * (par + 1)],
                                            gat_t[:, D * par:D * (par + 1)],
                                            pmax_all[:, kk:kk + 1], None,
                                            ALU.mult)
                        chain_v(v, i)
                        i = v.tensor_add(o9_t[:, D * par:D * (par + 1)],
                                         o9_t[:, D * par:D * (par + 1)],
                                         yf9_t[:, D * par:D * (par + 1)])
                        i.then_inc(s_va9, 1)
                        v.wait_ge(s_va9, 3 * kk + 1)
                        i = v.reduce_sum(sc9[:, so:so + 1],
                                         o9_t[:, D * par:D * (par + 1)],
                                         axis=AX.X)
                        chain_v(v, i)
                        i = v.tensor_scalar_mul(sc9[:, so + 1:so + 2],
                                                sc9[:, so:so + 1], 1.0 / D)
                        i.then_inc(s_va9, 1)
                        v.wait_ge(s_va9, 3 * kk + 2)
                        i = v.tensor_mul(sc9[:, so + 5:so + 6],
                                         sc9[:, so + 1:so + 2],
                                         sc9[:, so + 1:so + 2])
                        chain_v(v, i)
                        v.wait_ge(s_ssq9, kk + 1)
                        i = v.scalar_tensor_tensor(sc9[:, so + 2:so + 3],
                                                   sc9[:, so + 6:so + 7],
                                                   1.0 / D,
                                                   sc9[:, so + 5:so + 6],
                                                   ALU.mult, ALU.subtract)
                        i.then_inc(s_va9, 1)
                        v.wait_ge(s_rstd9, kk + 1)
                        i = v.scalar_tensor_tensor(
                            o9_t[:, D * par:D * (par + 1)],
                            o9_t[:, D * par:D * (par + 1)],
                            sc9[:, so + 1:so + 2],
                            sc9[:, so + 4:so + 5].broadcast_to([128, D]),
                            ALU.subtract, ALU.mult)
                        i.then_inc(s_o9, 1)

                @blk.scalar
                def _(s):
                    for kk in range(NT):
                        par = kk % 2
                        so = 16 * par
                        s.wait_ge(s_va9, 3 * kk + 1)
                        i = s.activation(sq9[:, :], o9_t[:, D * par:D * (par + 1)],
                                         AF.Square,
                                         accum_out=sc9[:, so + 6:so + 7])
                        i.then_inc(s_ssq9, 1)
                        s.wait_ge(s_va9, 3 * kk + 3)
                        i = s.activation(sc9[:, so + 2:so + 3],
                                         sc9[:, so + 2:so + 3], AF.Ln,
                                         bias=eps_t[:, :])
                        chain_a(s, i)
                        i = s.activation(sc9[:, so + 4:so + 5],
                                         sc9[:, so + 2:so + 3], AF.Exp,
                                         scale=-0.5)
                        i.then_inc(s_rstd9, 1)

            ctx9.close()

        if BUILD_STAGE < 4:
            ctx23.close()
        with nc.Block("pend", no_gpsimd_drain=True) as blk:

            @blk.gpsimd
            def _(g):
                g.wait_ge(s_sto2[0], 16 * (NT // 2))
                g.wait_ge(s_sto2[1], 16 * (NT // 2))
                g.wait_ge(s_sto2g[0], 32 * (NT // 2))
                g.wait_ge(s_sto2g[1], 32 * (NT // 2))
                if BUILD_STAGE >= 3:
                    g.wait_ge(s_attst[0], 32 * 16)
                    g.wait_ge(s_attst[1], 32 * 16)
                if BUILD_STAGE >= 4:
                    g.wait_ge(s_yst[0], 32 * 16)
                    g.wait_ge(s_yst[1], 32 * 16)
                if BUILD_STAGE >= 9:
                    g.wait_ge(s_ost9[0], 16 * (NT // 2))
                    g.wait_ge(s_ost9[1], 16 * (NT // 2))
                    if DEBUG:
                        i = g.dma_start(out=dbg_d[:, 512:544], in_=pmax_all[:, :])
                        i.then_inc(s_dbg, 16)
                        i = g.dma_start(out=dbg_d[:, 544:576], in_=lpos_all[:, :])
                        i.then_inc(s_dbg, 16)
                        i = g.dma_start(out=dbg_d[:, 576:608], in_=et_all[:, :])
                        i.then_inc(s_dbg, 16)
                        i = g.dma_start(out=dbg_d[:, 608:640], in_=slot_all[:, :])
                        i.then_inc(s_dbg, 16)
                        i = g.dma_start(out=dbg_d[:, 640:648], in_=b_sb[:, :])
                        i.then_inc(s_dbg, 16)
                        i = g.dma_start(out=dbg_d[0:8, 648:656],
                                        in_=cnt_sb[0:8, 0:8])
                        i.then_inc(s_dbg, 16)
                if DEBUG:
                    i = g.dma_start(out=dbg_d[:, 0:NT * H], in_=dq_all[:, :])
                    i.then_inc(s_dbg, 16)
                if BUILD_STAGE < 9:
                    for k in range(NT):
                        i = g.dma_start(
                            out=out_t[128 * k:128 * (k + 1), 0:D],
                            in_=zer_bf[:, :])
                        i.then_inc(s_out, 16)
                g.wait_ge(s_out, 16 * NT if BUILD_STAGE < 9 else 0)
                if DEBUG:
                    g.wait_ge(s_dbg, 16 * (7 if BUILD_STAGE >= 4 else 1))

    return nc


def _stage_inputs(L0_attn_input_seq_embs, L0_attn_W_Q, L0_attn_W_K,
                  L0_attn_W_V, L0_ffn_gate_w, L0_ffn_w1, L0_ffn_w2):
    x = np.ascontiguousarray(L0_attn_input_seq_embs.reshape(32768, 1024))
    xT = np.ascontiguousarray(x.T)
    op = np.arange(1024)
    perm = (op % 16) * 64 + op // 16
    wqt = np.ascontiguousarray(L0_attn_W_Q.T[:, perm])
    wkt = np.ascontiguousarray(L0_attn_W_K.T[:, perm])
    wvt = np.ascontiguousarray(L0_attn_W_V.T)
    gwr = np.ascontiguousarray(
        L0_ffn_gate_w.reshape(8, 128, 8).transpose(1, 0, 2).reshape(128, 64))
    in_maps = []
    for i in range(NCORES):
        in_maps.append({
            "x_in": x[T * i:T * (i + 1)],
            "xT_in": np.ascontiguousarray(xT[:, T * i:T * (i + 1)]),
            "wqt": wqt, "wkt": wkt, "wvt": wvt, "gwr": gwr,
            "w1": np.ascontiguousarray(L0_ffn_w1[i]),
            "w2": np.ascontiguousarray(L0_ffn_w2[i]),
            "meta": np.full((128, 1), float(i), dtype=np.float32),
        })
    return in_maps


def kernel(L0_attn_input_seq_embs, L0_attn_W_Q, L0_attn_W_K, L0_attn_W_V,
           L0_ffn_gate_w, L0_ffn_w1, L0_ffn_w2):
    in_maps = _stage_inputs(L0_attn_input_seq_embs, L0_attn_W_Q, L0_attn_W_K,
                            L0_attn_W_V, L0_ffn_gate_w, L0_ffn_w1, L0_ffn_w2)
    nc = build()
    res = run_bass_kernel_spmd(nc, in_maps, core_ids=list(range(NCORES)))
    out = np.concatenate([r["out"] for r in res.results], axis=0)
    return out.reshape(2, 16384, 1024)
